# version 04
# baseline (speedup 1.0000x reference)
import math
from contextlib import ExitStack

import numpy as np
import ml_dtypes

import concourse.bass as bass
import concourse.mybir as mybir
from concourse.bass_utils import run_bass_kernel_spmd

F32 = mybir.dt.float32
BF16 = mybir.dt.bfloat16
AF = mybir.ActivationFunctionType
ALU = mybir.AluOpType


class Cfg:
    def __init__(self, S=2048, D=2048, HQ=16, HKV=4, F=8192, TBLK=512, QBLK=512,
                 theta=1e4, eps=1e-5):
        self.S, self.D, self.HQ, self.HKV, self.F = S, D, HQ, HKV, F
        self.TBLK, self.QBLK = TBLK, QBLK
        self.theta, self.eps = theta, eps
        self.DH = 128
        self.ND = D // 128
        self.NF = F // 128
        self.NT = S // 128
        self.NTB = S // TBLK
        self.NQB = S // QBLK
        self.NDIAG = QBLK // 128
        self.G = HQ // HKV
        self.KVD = HKV * self.DH
        self.QKVW = (HQ + 2 * HKV) * self.DH
        self.NQKV = self.QKVW // 128
        assert D % 128 == 0 and F % 128 == 0 and S % TBLK == 0 and S % QBLK == 0
        assert TBLK <= 512 and QBLK <= 512 and QBLK % 128 == 0


class Ctr:
    def __init__(self, nc, ctx, name, step=1):
        self.h = ctx.enter_context(nc.semaphore(name))
        self.n = 0
        self.step = step

    def inc(self, instr, k=1):
        instr.then_inc(self.h, self.step * k)
        self.n += self.step * k
        return self.n

    def wait(self, eng, at=None):
        v = self.n if at is None else at
        if v > 0:
            eng.wait_ge(self.h, v)


def build_core_program(cfg: Cfg, phases=("ln1", "qkv", "attn", "ln3", "ffn"), stub=()):
    c = cfg
    nc = bass.Bass("TRN2", target_bir_lowering=False, detect_race_conditions=False)

    x_h = nc.dram_tensor("x", [c.ND, 128, c.S], F32, kind="ExternalInput")
    wqkv_h = nc.dram_tensor("wqkv_r", [c.NQKV, 128, c.ND, 128], F32, kind="ExternalInput")
    wo_h = nc.dram_tensor("wo_r", [c.ND, 128, c.HQ, 128], F32, kind="ExternalInput")
    wg_h = nc.dram_tensor("wg_r", [c.NF, 128, c.ND, 128], F32, kind="ExternalInput")
    wu_h = nc.dram_tensor("wu_r", [c.NF, 128, c.ND, 128], F32, kind="ExternalInput")
    wd_h = nc.dram_tensor("wd_r", [c.ND, 128, c.NF, 128], F32, kind="ExternalInput")
    c1_h = nc.dram_tensor("rope_c1", [128, c.S], F32, kind="ExternalInput")
    c2_h = nc.dram_tensor("rope_c2", [128, c.S], F32, kind="ExternalInput")
    mask_h = nc.dram_tensor("masks", [c.NDIAG, 128, c.QBLK], BF16, kind="ExternalInput")
    out_h = nc.dram_tensor("out", [c.ND, 128, c.S], F32, kind="ExternalOutput")

    q_sc = nc.dram_tensor("q_sc", [c.HQ, 128, c.S], BF16)
    dbg_q = dbg_x3 = dbg_wb = None
    if "ffn" not in phases:
        dbg_q = nc.dram_tensor("dbg_q", [c.HQ, 128, c.S], BF16, kind="ExternalOutput")
        dbg_x3 = nc.dram_tensor("dbg_x3", [c.ND, 128, c.S], BF16, kind="ExternalOutput")
        dbg_wb = nc.dram_tensor("dbg_wb", [128, c.ND * 128], BF16, kind="ExternalOutput")
        dbg_p = nc.dram_tensor("dbg_p", [128, c.G * c.NT * c.QBLK], BF16, kind="ExternalOutput")
        dbg_af = nc.dram_tensor("dbg_af", [128, c.HQ * c.QBLK], BF16, kind="ExternalOutput")
        dbg_rl = nc.dram_tensor("dbg_rl", [1, 2 * c.QBLK], BF16, kind="ExternalOutput")
    x2_sc = nc.dram_tensor("x2_sc", [c.ND, 128, c.S], F32)
    xn3_sc = nc.dram_tensor("xn3_sc", [c.ND, 128, c.S], BF16)

    inv_d = 1.0 / c.D
    sc_scale = 1.0 / math.sqrt(c.DH)

    M = {}

    def mark(key, ctr):
        M[key] = (ctr, ctr.n)

    def wkey(eng, key):
        ctr, v = M[key]
        if v > 0:
            eng.wait_ge(ctr.h, v)

    ctx = ExitStack()
    with ctx:
        ones_col = ctx.enter_context(nc.sbuf_tensor("ones_col", [128, 1], BF16))
        ones_row = ctx.enter_context(nc.sbuf_tensor("ones_row", [1, 128], F32))
        ones_row_bf = ctx.enter_context(nc.sbuf_tensor("ones_row_bf", [1, 128], BF16))

        dl = Ctr(nc, ctx, "dl", 16)
        ds = Ctr(nc, ctx, "ds", 16)
        dg = Ctr(nc, ctx, "dg", 16)
        st = Ctr(nc, ctx, "st", 1)
        ss = Ctr(nc, ctx, "ss", 1)
        sv = Ctr(nc, ctx, "sv", 1)
        tA = Ctr(nc, ctx, "tA", 1)
        sB = Ctr(nc, ctx, "sB", 1)
        sD1 = Ctr(nc, ctx, "sD1", 1)
        sD2 = Ctr(nc, ctx, "sD2", 1)
        vC1 = Ctr(nc, ctx, "vC1", 1)
        vC2 = Ctr(nc, ctx, "vC2", 1)
        vC3 = Ctr(nc, ctx, "vC3", 1)
        vC4 = Ctr(nc, ctx, "vC4", 1)
        vC6 = Ctr(nc, ctx, "vC6", 1)
        vC7 = Ctr(nc, ctx, "vC7", 1)
        ALLC = [dl, ds, dg, st, ss, sv, tA, sB, sD1, sD2, vC1, vC2, vC3, vC4, vC6, vC7]

        block = ctx.enter_context(nc.Block())

        def phase_snap():
            return [(ctr, ctr.n) for ctr in ALLC]

        def make_join(snap):
            def join_all(eng):
                for ctr, v in snap:
                    ctr.wait(eng, v)
            return join_all

        @block.vector
        def _(vector):
            vector.memset(ones_col[:, :], 1.0)
            vector.memset(ones_row[:, :], 1.0)
            vector.memset(ones_row_bf[:, :], 1.0)
            sv.inc(vector.drain(fusable=False))
        mark("ones", sv)

        def emit_ln(tag, src_hbm, xn1_dst, dst_hbm):
            T = c.TBLK
            NTB = c.NTB
            resident = xn1_dst is not None
            join_all = make_join(phase_snap())
            with ExitStack() as px:
                x_s = px.enter_context(nc.sbuf_tensor(f"{tag}_x", [128, c.ND * T], F32))
                x_bf = px.enter_context(nc.sbuf_tensor(f"{tag}_xbf", [128, c.ND * T], BF16))
                x_sq = px.enter_context(nc.sbuf_tensor(f"{tag}_xsq", [128, c.ND * T], BF16))
                rows = px.enter_context(nc.sbuf_tensor(f"{tag}_rows", [1, 4 * T], F32))
                ost = (None if resident else
                       px.enter_context(nc.sbuf_tensor(f"{tag}_o", [128, c.ND * T], BF16)))
                ps_st0 = px.enter_context(nc.psum_tensor(f"{tag}_s0", [1, T], F32))
                ps_st1 = px.enter_context(nc.psum_tensor(f"{tag}_s1", [1, T], F32))
                ps_bc1 = px.enter_context(nc.psum_tensor(f"{tag}_b1", [128, T], F32))
                ps_bc2 = px.enter_context(nc.psum_tensor(f"{tag}_b2", [128, T], F32))

                vA0, sB0, vC60, sD20, vC30, vC70 = (vC1.n, sB.n, vC6.n, sD2.n, vC3.n, vC7.n)

                @block.sync
                def _(sync):
                    join_all(sync)
                    for tb in range(NTB):
                        if not resident and tb >= 1:
                            for d in range(c.ND):
                                vC7.wait(sync, vC70 + (tb - 1) * c.ND + d + 1)
                                i = sync.dma_start(
                                    out=dst_hbm[d, :, (tb - 1) * T: tb * T],
                                    in_=ost[:, d * T:(d + 1) * T])
                                dl.inc(i)
                                mark((tag, "store", tb - 1, d), dl)
                        vC1.wait(sync, vA0 + tb)
                        i = sync.dma_start(
                            out=bass.AP(x_s, 0, [[c.ND * T, 128], [T, c.ND], [1, T]]),
                            in_=bass.AP(src_hbm, tb * T,
                                        [[c.S, 128], [128 * c.S, c.ND], [1, T]]))
                        dl.inc(i)
                        mark((tag, "load", tb), dl)
                    if not resident:
                        for d in range(c.ND):
                            vC7.wait(sync, vC70 + (NTB - 1) * c.ND + d + 1)
                            i = sync.dma_start(
                                out=dst_hbm[d, :, (NTB - 1) * T: NTB * T],
                                in_=ost[:, d * T:(d + 1) * T])
                            dl.inc(i)
                            mark((tag, "store", NTB - 1, d), dl)

                @block.tensor
                def _(tensor):
                    join_all(tensor)
                    for tb in range(NTB):
                        sB.wait(tensor, sB0 + tb + 1)
                        if tb >= 1:
                            vC6.wait(tensor, vC60 + tb)
                            vC1.wait(tensor, vA0 + tb)
                        for d in range(c.ND):
                            i = tensor.matmul(ps_st0[:, :], ones_col[:, :],
                                              x_bf[:, d * T:(d + 1) * T],
                                              start=(d == 0), stop=(d == c.ND - 1))
                        st.inc(tensor.drain(fusable=False))
                        mark((tag, "st0", tb), st)
                        vC3.wait(tensor, vC30 + tb + 1)
                        for d in range(c.ND):
                            i = tensor.matmul(ps_st1[:, :], ones_col[:, :],
                                              x_sq[:, d * T:(d + 1) * T],
                                              start=(d == 0), stop=(d == c.ND - 1))
                        st.inc(tensor.drain(fusable=False))
                        mark((tag, "st1", tb), st)
                        vC6.wait(tensor, vC60 + tb + 1)
                        sD2.wait(tensor, sD20 + tb + 1)
                        i = tensor.matmul(ps_bc1[:, :], ones_row[:, :], rows[:, 0:T],
                                          start=True, stop=True)
                        st.inc(tensor.drain(fusable=False))
                        i = tensor.matmul(ps_bc2[:, :], ones_row[:, :],
                                          rows[:, 2 * T:3 * T], start=True, stop=True)
                        st.inc(tensor.drain(fusable=False))
                        mark((tag, "bc", tb), st)

                @block.scalar
                def _(scalar):
                    join_all(scalar)
                    for tb in range(NTB):
                        wkey(scalar, (tag, "load", tb))
                        if tb >= 1:
                            wkey(scalar, (tag, "st0", tb - 1))
                            vC3.wait(scalar, vC30 + tb)
                        for d in range(c.ND):
                            i = scalar.copy(x_bf[:, d * T:(d + 1) * T],
                                            x_s[:, d * T:(d + 1) * T])
                        sB.inc(scalar.drain(fusable=False))
                        vC6.wait(scalar, vC60 + tb + 1)
                        scalar.activation(rows[:, 3 * T:4 * T],
                                          rows[:, 2 * T:3 * T], AF.Ln)
                        scalar.drain(fusable=False)
                        i = scalar.activation(rows[:, 2 * T:3 * T], rows[:, 3 * T:4 * T],
                                              AF.Exp, scale=-0.5)
                        sD2.inc(scalar.drain(fusable=False))

                @block.vector
                def _(vector):
                    join_all(vector)
                    for tb in range(NTB):
                        sB.wait(vector, sB0 + tb + 1)
                        if tb >= 1:
                            wkey(vector, (tag, "st1", tb - 1))
                        for d in range(c.ND):
                            i = vector.tensor_mul(x_sq[:, d * T:(d + 1) * T],
                                                  x_bf[:, d * T:(d + 1) * T],
                                                  x_bf[:, d * T:(d + 1) * T])
                        vC3.inc(vector.drain(fusable=False))
                        wkey(vector, (tag, "st1", tb))
                        if tb >= 1:
                            sD2.wait(vector, sD20 + tb)
                        vector.tensor_scalar_mul(rows[:, 0:T], ps_st0[:, :], inv_d)
                        vector.tensor_scalar_mul(rows[:, T:2 * T], ps_st1[:, :], inv_d)
                        vector.tensor_mul(rows[:, 3 * T:4 * T], rows[:, 0:T], rows[:, 0:T])
                        vector.tensor_sub(rows[:, 2 * T:3 * T], rows[:, T:2 * T],
                                          rows[:, 3 * T:4 * T])
                        i = vector.tensor_scalar_add(rows[:, 2 * T:3 * T],
                                                     rows[:, 2 * T:3 * T], c.eps)
                        vC6.inc(vector.drain(fusable=False))
                        wkey(vector, (tag, "bc", tb))
                        for d in range(c.ND):
                            vector.tensor_sub(x_s[:, d * T:(d + 1) * T],
                                              x_s[:, d * T:(d + 1) * T], ps_bc1[:, :])
                            if resident:
                                dst = xn1_dst[:, d * c.S + tb * T: d * c.S + tb * T + T]
                            else:
                                dst = ost[:, d * T:(d + 1) * T]
                                if tb >= 1:
                                    wkey(vector, (tag, "store", tb - 1, d))
                            i = vector.tensor_tensor(dst, x_s[:, d * T:(d + 1) * T],
                                                     ps_bc2[:, :], ALU.mult)
                            if not resident:
                                vC7.inc(vector.drain(fusable=False))
                        vC1.inc(vector.drain(fusable=False))

        def emit_qkv(xn1, kcache, vcache):
            join_all = make_join(phase_snap())
            def ST(name):
                return name in stub
            TB = min(512, c.S)
            NB = c.S // TB
            NQK = c.HQ + c.HKV
            forder = list(range(NQK))[::-1] if ST("revf") else list(range(NQK))
            qpos = [fi * NB + tb for fi, f in enumerate(forder)
                    for tb in range(NB) if f < c.HQ]
            qprev = {idx: (qpos[i - 2] if i >= 2 else None)
                     for i, idx in enumerate(qpos)}
            with ExitStack() as px:
                wbuf = px.enter_context(nc.sbuf_tensor("q_w", [128, 2 * c.ND * 128], BF16))
                vwbuf = px.enter_context(nc.sbuf_tensor("q_vw", [128, c.ND * c.KVD], BF16))
                p_s = px.enter_context(nc.sbuf_tensor("q_ps", [128, 2 * TB], F32))
                p_w = px.enter_context(nc.sbuf_tensor("q_pw", [128, 2 * TB], F32))
                m1 = px.enter_context(nc.sbuf_tensor("q_m1", [128, 2 * TB], F32))
                qstage = px.enter_context(nc.sbuf_tensor("q_qs", [128, 2 * TB], BF16))
                c1s = px.enter_context(nc.sbuf_tensor("q_c1", [128, c.S], F32))
                c2s = px.enter_context(nc.sbuf_tensor("q_c2", [128, c.S], F32))
                ps_qk = [px.enter_context(nc.psum_tensor(f"q_p{i}", [128, TB], F32))
                         for i in range(2)]
                ps_v = [px.enter_context(nc.psum_tensor(f"q_v{i}", [128, c.KVD], F32))
                        for i in range(2)]

                tA0, sB0, vC10, vC20 = tA.n, sB.n, vC1.n, vC2.n

                @block.gpsimd
                def _(gpsimd):
                    join_all(gpsimd)
                    for fi, f in enumerate(forder):
                        if fi >= 2:
                            tA.wait(gpsimd, tA0 + fi - 1)
                        i = gpsimd.dma_start(
                            out=bass.AP(wbuf, (fi % 2) * c.ND * 128,
                                        [[2 * c.ND * 128, 128], [128, c.ND], [1, 128]]),
                            in_=bass.AP(wqkv_h, f * c.D * 128,
                                        [[c.ND * 128, 128], [128, c.ND], [1, 128]]))
                        dg.inc(i)
                        mark(("qkv_w", fi), dg)
                    for b in range(c.HKV):
                        i = gpsimd.dma_start(
                            out=bass.AP(vwbuf, b * 128,
                                        [[c.ND * c.KVD, 128], [c.KVD, c.ND], [1, 128]]),
                            in_=bass.AP(wqkv_h, (NQK + b) * c.D * 128,
                                        [[c.ND * 128, 128], [128, c.ND], [1, 128]]))
                        dg.inc(i)
                    mark(("qkv_vw",), dg)

                @block.sync
                def _(sync):
                    join_all(sync)
                    i = sync.dma_start(out=c1s[:, :], in_=c1_h[:, :])
                    dl.inc(i)
                    i = sync.dma_start(out=c2s[:, :], in_=c2_h[:, :])
                    dl.inc(i)
                    mark(("rope_tabs",), dl)
                    for fi, f in enumerate(forder):
                        for tb in range(NB):
                            idx = fi * NB + tb
                            sl = slice((idx % 2) * TB, (idx % 2) * TB + TB)
                            sB.wait(sync, sB0 + idx + 1)
                            if idx >= 2:
                                vC1.wait(sync, vC10 + idx - 1)
                            i = sync.dma_start(out=p_w[0:64, sl], in_=p_s[64:128, sl])
                            dl.inc(i)
                            i = sync.dma_start(out=p_w[64:128, sl], in_=p_s[0:64, sl])
                            dl.inc(i)
                            mark(("swap", idx), dl)
                            if f < c.HQ:
                                vC1.wait(sync, vC10 + idx + 1)
                                i = sync.dma_start(
                                    out=q_sc[f, :, tb * TB:(tb + 1) * TB],
                                    in_=qstage[:, sl])
                                dl.inc(i)
                                mark(("qstore", idx), dl)

                @block.tensor
                def _(tensor):
                    join_all(tensor)
                    for fi, f in enumerate(forder):
                        wkey(tensor, ("qkv_w", fi + 1) if fi + 1 < NQK else ("qkv_vw",))
                        if ST("delay") and fi == 0:
                            tensor.nop(cycle_cnt=50000)
                        for tb in range(NB):
                            idx = fi * NB + tb
                            if idx >= 2:
                                sB.wait(tensor, sB0 + idx - 1)
                            for d in range(c.ND):
                                i = tensor.matmul(
                                    ps_qk[idx % 2][:, :],
                                    wbuf[:, (fi % 2) * c.ND * 128 + d * 128:
                                         (fi % 2) * c.ND * 128 + (d + 1) * 128],
                                    xn1[:, d * c.S + tb * TB: d * c.S + tb * TB + TB],
                                    start=(d == 0), stop=(d == c.ND - 1))
                            st.inc(tensor.drain(fusable=False))
                            mark(("qk_mm", idx), st)
                        tA.inc(tensor.drain(fusable=False))
                    wkey(tensor, ("qkv_vw",))
                    for ts in range(c.NT):
                        if ts >= 2:
                            vC2.wait(tensor, vC20 + ts - 1)
                        for d in range(c.ND):
                            i = tensor.matmul(
                                ps_v[ts % 2][:, :],
                                xn1[:, d * c.S + ts * 128: d * c.S + ts * 128 + 128],
                                vwbuf[:, d * c.KVD:(d + 1) * c.KVD],
                                start=(d == 0), stop=(d == c.ND - 1))
                        st.inc(tensor.drain(fusable=False))
                        mark(("v_mm", ts), st)

                @block.scalar
                def _(scalar):
                    join_all(scalar)
                    for fi, f in enumerate(forder):
                        for tb in range(NB):
                            idx = fi * NB + tb
                            sl = slice((idx % 2) * TB, (idx % 2) * TB + TB)
                            wkey(scalar, ("qk_mm", idx))
                            if idx >= 2:
                                vC1.wait(scalar, vC10 + idx - 1)
                            i = scalar.copy(p_s[:, sl], ps_qk[idx % 2][:, :])
                            sB.inc(scalar.drain(fusable=False))

                @block.vector
                def _(vector):
                    join_all(vector)
                    wkey(vector, ("rope_tabs",))
                    for fi, f in enumerate(forder):
                        for tb in range(NB):
                            idx = fi * NB + tb
                            sl = slice((idx % 2) * TB, (idx % 2) * TB + TB)
                            if ST("ropecopy0"):
                                wkey(vector, ("qk_mm", idx))
                            wkey(vector, ("swap", idx))
                            if f < c.HQ and qprev[idx] is not None:
                                wkey(vector, ("qstore", qprev[idx]))
                            if f < c.HQ:
                                dst = qstage[:, sl]
                            else:
                                kv = f - c.HQ
                                dst = kcache[:, kv * c.S + tb * TB: kv * c.S + tb * TB + TB]
                            if ST("ropecopy0"):
                                i = vector.tensor_copy(dst, ps_qk[idx % 2][:, :])
                            elif ST("ropecopy1"):
                                i = vector.tensor_copy(dst, p_s[:, sl])
                            elif ST("ropecopy2"):
                                i = vector.tensor_copy(dst, p_w[:, sl])
                            else:
                                vector.tensor_mul(m1[:, sl], p_s[:, sl],
                                                  c1s[:, tb * TB:(tb + 1) * TB])
                                vector.tensor_mul(p_w[:, sl], p_w[:, sl],
                                                  c2s[:, tb * TB:(tb + 1) * TB])
                                i = vector.tensor_add(dst, m1[:, sl], p_w[:, sl])
                            vC1.inc(vector.drain(fusable=False))
                    for ts in range(c.NT):
                        wkey(vector, ("v_mm", ts))
                        i = vector.tensor_copy(vcache[:, ts * c.KVD:(ts + 1) * c.KVD],
                                               ps_v[ts % 2][:, :])
                        vC2.inc(vector.drain(fusable=False))

        def emit_attn(kcache, vcache):
            join_all = make_join(phase_snap())
            def ST(name):
                return name in stub
            Q = c.QBLK
            with ExitStack() as px:
                qslab = px.enter_context(nc.sbuf_tensor("a_q", [128, c.HQ * Q], BF16))
                probs = px.enter_context(
                    nc.sbuf_tensor("a_p", [128, c.G * c.NT * Q], BF16))
                attnf = px.enter_context(nc.sbuf_tensor("a_af", [128, c.HQ * Q], BF16))
                masks = px.enter_context(
                    nc.sbuf_tensor("a_mk", [128, c.NDIAG * Q], BF16))
                lrow = px.enter_context(nc.sbuf_tensor("a_l", [1, Q], F32))
                rlbf = px.enter_context(nc.sbuf_tensor("a_rl", [1, 2 * Q], BF16))
                rlbc = px.enter_context(nc.sbuf_tensor("a_rlbc", [128, 2 * Q], BF16))
                wobuf = px.enter_context(
                    nc.sbuf_tensor("a_wo", [128, 2 * c.HQ * 128], BF16))
                wo_sb = px.enter_context(nc.sbuf_tensor("a_wosb", [128, c.ND * Q], F32))
                canary = px.enter_context(nc.sbuf_tensor("a_can", [128, 16], BF16))
                xst = px.enter_context(nc.sbuf_tensor("a_x", [128, 2 * Q], F32))
                x2st = px.enter_context(nc.sbuf_tensor("a_x2", [128, 2 * Q], F32))
                rows = px.enter_context(nc.sbuf_tensor("a_rows", [1, 4 * Q], F32))

                ps_sc = [px.enter_context(nc.psum_tensor(f"a_sc{i}", [128, Q], F32))
                         for i in range(2)]
                ps_l = [px.enter_context(nc.psum_tensor(f"a_l{i}", [1, Q], F32))
                        for i in range(2)]
                ps_pv = px.enter_context(nc.psum_tensor("a_pv", [128, Q], F32))
                ps_bc = px.enter_context(nc.psum_tensor("a_bc", [128, Q], F32))
                ps_wo = [px.enter_context(nc.psum_tensor(f"a_wo{i}", [128, Q], F32))
                         for i in range(2)]
                assert 2 * c.ND <= c.G * c.NT, "probs scratch too small for wo_bf"
                wo_bf = probs[:, c.ND * Q: 2 * c.ND * Q]

                tA0, sB0, sD10, sD20 = tA.n, sB.n, sD1.n, sD2.n
                vC10, vC20, vC30, vC40, vC60, vC70 = (vC1.n, vC2.n, vC3.n,
                                                      vC4.n, vC6.n, vC7.n)

                def kcnt(qb):
                    return (qb + 1) * c.NDIAG

                def exps_before(qb):
                    return c.HQ * sum(kcnt(i) for i in range(qb))

                def vC1_base(qb):
                    return vC10 + sum(c.HQ * c.NDIAG + c.ND for _ in range(qb)) * 0 + \
                        qb * (c.HQ * c.NDIAG + c.ND)

                @block.gpsimd
                def _(gpsimd):
                    join_all(gpsimd)
                    for qb in range(c.NQB):
                        for d in range(c.ND):
                            gidx = qb * c.ND + d
                            if gidx >= 2:
                                tA.wait(gpsimd, tA0 + gidx - 1)
                            if ST("dwo"):
                                i = gpsimd.dma_start(
                                    out=wobuf[:, 0:128],
                                    in_=bass.AP(wo_h, 0, [[c.HQ * 128, 128], [1, 128]]))
                            else:
                                i = gpsimd.dma_start(
                                    out=bass.AP(wobuf, (d % 2) * c.HQ * 128,
                                                [[2 * c.HQ * 128, 128], [128, c.HQ], [1, 128]]),
                                    in_=bass.AP(wo_h, d * c.HQ * c.DH * 128,
                                                [[c.HQ * 128, 128], [128, c.HQ], [1, 128]]))
                            dg.inc(i)
                            mark(("wo_w", gidx), dg)
                    i = gpsimd.dma_start(
                        out=canary[:, :],
                        in_=bass.AP(wo_h, (c.ND - 1) * c.HQ * c.DH * 128,
                                    [[c.HQ * 128, 128], [1, 16]]))
                    dg.inc(i)
                    mark(("wo_can",), dg)

                @block.sync
                def _(sync):
                    join_all(sync)
                    if ST("dmask"):
                        i = sync.dma_start(out=masks[:, 0:Q],
                                           in_=bass.AP(mask_h, 0, [[Q, 128], [1, Q]]))
                    else:
                        i = sync.dma_start(
                            out=bass.AP(masks, 0,
                                        [[c.NDIAG * Q, 128], [Q, c.NDIAG], [1, Q]]),
                            in_=bass.AP(mask_h, 0,
                                        [[Q, 128], [128 * Q, c.NDIAG], [1, Q]]))
                    dl.inc(i)
                    mark(("masks",), dl)
                    for qb in range(c.NQB):
                        if qb >= 1:
                            tA.wait(sync, tA0 + (qb - 1) * c.ND + 1)
                        if ST("dqslab"):
                            i = sync.dma_start(out=qslab[:, 0:Q],
                                               in_=bass.AP(q_sc, 0, [[c.S, 128], [1, Q]]))
                        else:
                            i = sync.dma_start(
                                out=bass.AP(qslab, 0, [[c.HQ * Q, 128], [Q, c.HQ], [1, Q]]),
                                in_=bass.AP(q_sc, qb * Q,
                                            [[c.S, 128], [128 * c.S, c.HQ], [1, Q]]))
                        dl.inc(i)
                        mark(("qslab", qb), dl)
                        if ST("dumpp") and qb == c.NQB - 1:
                            vC3.wait(sync, vC30 + c.NQB * c.HQ)
                            i = sync.dma_start(out=dbg_p[:, :], in_=probs[:, :])
                            dl.inc(i)
                            i = sync.dma_start(out=dbg_af[:, :], in_=attnf[:, :])
                            dl.inc(i)
                            i = sync.dma_start(out=dbg_rl[:, :], in_=rlbf[:, :])
                            dl.inc(i)
                            mark(("pdump",), dl)
                        for d in range(c.ND):
                            gi = qb * c.ND + d
                            if gi >= 2:
                                vC7.wait(sync, vC70 + gi - 1)
                            if ST("dx"):
                                i = sync.dma_start(out=xst[:, 0:Q],
                                                   in_=x_h[0, :, 0:Q])
                            else:
                                i = sync.dma_start(out=xst[:, (d % 2) * Q:(d % 2) * Q + Q],
                                                   in_=x_h[d, :, qb * Q:(qb + 1) * Q])
                            dl.inc(i)
                            mark(("a_x", qb, d), dl)
                            vC7.wait(sync, vC70 + gi + 1)
                            if ST("dx2"):
                                i = sync.dma_start(out=x2_sc[0, :, 0:Q], in_=x2st[:, 0:Q])
                            else:
                                i = sync.dma_start(out=x2_sc[d, :, qb * Q:(qb + 1) * Q],
                                                   in_=x2st[:, (d % 2) * Q:(d % 2) * Q + Q])
                            dl.inc(i)
                            mark(("a_x2", qb, d), dl)

                @block.tensor
                def _(tensor):
                    join_all(tensor)
                    for qb in range(c.NQB):
                        kc = kcnt(qb)
                        wkey(tensor, ("qslab", qb))
                        if qb >= 1:
                            vC7.wait(tensor, vC70 + qb * c.ND)
                            vC6.wait(tensor, vC60 + qb)
                        for kv in range(c.HKV):
                            for j in range(c.G):
                                h = kv * c.G + j
                                ho = qb * c.HQ + h
                                for kj in range(kc):
                                    eidx = sB0 + exps_before(qb) + h * kc + kj
                                    if kj >= 2:
                                        sB.wait(tensor, eidx - 2 + 1)
                                    if ST("sc"):
                                        i = tensor.nop()
                                    else:
                                        i = tensor.matmul(
                                            ps_sc[kj % 2][:, :],
                                            kcache[:, kv * c.S + kj * 128:
                                                   kv * c.S + (kj + 1) * 128],
                                            qslab[:, h * Q:(h + 1) * Q],
                                            start=True, stop=True)
                                    st.inc(tensor.drain(fusable=False))
                                    mark(("sc", qb, h, kj), st)
                                sB.wait(tensor, sB0 + exps_before(qb) + (h + 1) * kc)
                                vC1.wait(tensor, vC1_base(qb) + (h + 1) * c.NDIAG)
                                if h >= 2:
                                    vC2.wait(tensor, vC20 + qb * c.HQ + h - 1)
                                if ST("l"):
                                    i = tensor.nop()
                                else:
                                    for kj in range(kc):
                                        i = tensor.matmul(
                                            ps_l[j % 2][:, :], ones_col[:, :],
                                            probs[:, (kj * c.G + j) * Q:(kj * c.G + j + 1) * Q],
                                            start=(kj == 0), stop=(kj == kc - 1))
                                st.inc(tensor.drain(fusable=False))
                                mark(("lmm", qb, h), st)
                                if h >= 1:
                                    vC3.wait(tensor, vC30 + qb * c.HQ + h)
                                if ST("pv"):
                                    i = tensor.nop()
                                else:
                                    for kj in range(kc):
                                        i = tensor.matmul(
                                            ps_pv[:, :],
                                            vcache[:, kj * c.KVD + kv * 128:
                                                   kj * c.KVD + kv * 128 + 128],
                                            probs[:, (kj * c.G + j) * Q:(kj * c.G + j + 1) * Q],
                                            start=(kj == 0), stop=(kj == kc - 1))
                                st.inc(tensor.drain(fusable=False))
                                mark(("pv", qb, h), st)
                                vC2.wait(tensor, vC20 + qb * c.HQ + h + 1)
                                if ST("bc"):
                                    i = tensor.nop()
                                else:
                                    i = tensor.matmul(ps_bc[:, :], ones_row_bf[:, :],
                                                      rlbf[:, (j % 2) * Q:(j % 2) * Q + Q],
                                                      start=True, stop=True)
                                st.inc(tensor.drain(fusable=False))
                                mark(("lbc", qb, h), st)
                        vC3.wait(tensor, vC30 + (qb + 1) * c.HQ)
                        for d in range(c.ND):
                            gidx = qb * c.ND + d
                            if gidx + 1 < c.NQB * c.ND:
                                wkey(tensor, ("wo_w", gidx + 1))
                            else:
                                wkey(tensor, ("wo_can",))
                            if d >= 2:
                                vC4.wait(tensor, vC40 + qb * c.ND + d - 1)
                                sD1.wait(tensor, sD10 + qb * c.ND + d - 1)
                            if ST("wo"):
                                i = tensor.nop()
                            else:
                                for dh in range(c.HQ):
                                    i = tensor.matmul(
                                        ps_wo[d % 2][:, :],
                                        wobuf[:, (d % 2) * c.HQ * 128 + dh * 128:
                                              (d % 2) * c.HQ * 128 + (dh + 1) * 128],
                                        attnf[:, dh * Q:(dh + 1) * Q],
                                        start=(dh == 0), stop=(dh == c.HQ - 1))
                            st.inc(tensor.drain(fusable=False))
                            mark(("womm", qb, d), st)
                            tA.inc(tensor.drain(fusable=False))
                        if ST("dumpp") and qb == c.NQB - 1:
                            wkey(tensor, ("pdump",))
                        sD1.wait(tensor, sD10 + (qb + 1) * c.ND)
                        if ST("ln2mm"):
                            i = tensor.nop()
                        else:
                            for d in range(c.ND):
                                i = tensor.matmul(ps_l[0][:, :], ones_col[:, :],
                                                  probs[:, (c.ND + d) * Q:(c.ND + d + 1) * Q],
                                                  start=(d == 0), stop=(d == c.ND - 1))
                        st.inc(tensor.drain(fusable=False))
                        mark(("ln2_s0", qb), st)
                        vC1.wait(tensor, vC1_base(qb) + c.HQ * c.NDIAG + c.ND)
                        if ST("ln2mm"):
                            i = tensor.nop()
                        else:
                            for d in range(c.ND):
                                i = tensor.matmul(ps_l[1][:, :], ones_col[:, :],
                                                  probs[:, d * Q:(d + 1) * Q],
                                                  start=(d == 0), stop=(d == c.ND - 1))
                        st.inc(tensor.drain(fusable=False))
                        mark(("ln2_s1", qb), st)
                        vC6.wait(tensor, vC60 + qb + 1)
                        sD2.wait(tensor, sD20 + qb + 1)
                        if ST("ln2mm"):
                            st.inc(tensor.nop())
                            st.inc(tensor.nop())
                        else:
                            i = tensor.matmul(ps_sc[0][:, :], ones_row[:, :], rows[:, 0:Q],
                                              start=True, stop=True)
                            st.inc(tensor.drain(fusable=False))
                            i = tensor.matmul(ps_sc[1][:, :], ones_row[:, :],
                                              rows[:, 2 * Q:3 * Q], start=True, stop=True)
                            st.inc(tensor.drain(fusable=False))
                        mark(("ln2_bc", qb), st)

                @block.scalar
                def _(scalar):
                    join_all(scalar)
                    for qb in range(c.NQB):
                        kc = kcnt(qb)
                        if qb >= 1:
                            wkey(scalar, ("ln2_s1", qb - 1))
                        for kv in range(c.HKV):
                            for j in range(c.G):
                                h = kv * c.G + j
                                for kj in range(kc):
                                    wkey(scalar, ("sc", qb, h, kj))
                                    if ST("exp"):
                                        i = scalar.nop()
                                    else:
                                        i = scalar.activation(
                                            probs[:, (kj * c.G + j) * Q:
                                                  (kj * c.G + j + 1) * Q],
                                            ps_sc[kj % 2][:, :], AF.Exp, scale=sc_scale)
                                    sB.inc(scalar.drain(fusable=False))
                        if ST("dumpp") and qb == c.NQB - 1:
                            wkey(scalar, ("pdump",))
                        for d in range(c.ND):
                            vC4.wait(scalar, vC40 + qb * c.ND + d + 1)
                            if ST("swo"):
                                i = scalar.nop()
                            else:
                                i = scalar.copy(probs[:, (c.ND + d) * Q:(c.ND + d + 1) * Q],
                                                wo_sb[:, d * Q:(d + 1) * Q])
                            sD1.inc(scalar.drain(fusable=False))
                        vC6.wait(scalar, vC60 + qb + 1)
                        if ST("srows"):
                            i = scalar.nop()
                        else:
                            scalar.activation(rows[:, 3 * Q:4 * Q],
                                              rows[:, 2 * Q:3 * Q], AF.Ln)
                            scalar.drain(fusable=False)
                            i = scalar.activation(rows[:, 2 * Q:3 * Q], rows[:, 3 * Q:4 * Q],
                                                  AF.Exp, scale=-0.5)
                        sD2.inc(scalar.drain(fusable=False))

                @block.vector
                def _(vector):
                    join_all(vector)
                    wkey(vector, ("masks",))
                    for qb in range(c.NQB):
                        kc = kcnt(qb)
                        for kv in range(c.HKV):
                            for j in range(c.G):
                                h = kv * c.G + j
                                for m in range(c.NDIAG):
                                    kj = qb * c.NDIAG + m
                                    eidx = sB0 + exps_before(qb) + h * kc + kj
                                    sB.wait(vector, eidx + 1)
                                    psl = probs[:, (kj * c.G + j) * Q:
                                                (kj * c.G + j + 1) * Q]
                                    if ST("vmask"):
                                        i = vector.engine_nop()
                                    else:
                                        i = vector.tensor_mul(psl, psl,
                                                              masks[:, m * Q:(m + 1) * Q])
                                    vC1.inc(vector.drain(fusable=False))
                                wkey(vector, ("lmm", qb, h))
                                ls = slice((j % 2) * Q, (j % 2) * Q + Q)
                                if ST("vl"):
                                    i = vector.engine_nop()
                                else:
                                    vector.tensor_copy(lrow[:, 0:Q], ps_l[j % 2][:, :])
                                    vector.reciprocal(lrow[:, 0:Q], lrow[:, 0:Q])
                                    vector.drain(fusable=False)
                                    i = vector.tensor_copy(rlbf[:, ls], lrow[:, 0:Q])
                                vC2.inc(vector.drain(fusable=False))
                                wkey(vector, ("lbc", qb, h))
                                wkey(vector, ("pv", qb, h))
                                if ST("vevac"):
                                    i = vector.engine_nop()
                                else:
                                    vector.tensor_copy(rlbc[:, ls], ps_bc[:, :])
                                    i = vector.tensor_tensor(attnf[:, h * Q:(h + 1) * Q],
                                                             ps_pv[:, :], rlbc[:, ls],
                                                             ALU.mult)
                                vC3.inc(vector.drain(fusable=False))
                        for d in range(c.ND):
                            wkey(vector, ("womm", qb, d))
                            if ST("vwo"):
                                i = vector.engine_nop()
                            else:
                                i = vector.tensor_copy(wo_sb[:, d * Q:(d + 1) * Q],
                                                       ps_wo[d % 2][:, :])
                            vC4.inc(vector.drain(fusable=False))
                        if ST("dumpp") and qb == c.NQB - 1:
                            wkey(vector, ("pdump",))
                        for d in range(c.ND):
                            sD1.wait(vector, sD10 + qb * c.ND + d + 1)
                            if ST("vsq"):
                                i = vector.engine_nop()
                            else:
                                i = vector.tensor_mul(probs[:, d * Q:(d + 1) * Q],
                                                      probs[:, (c.ND + d) * Q:(c.ND + d + 1) * Q],
                                                      probs[:, (c.ND + d) * Q:(c.ND + d + 1) * Q])
                            vC1.inc(vector.drain(fusable=False))
                        wkey(vector, ("ln2_s1", qb))
                        if qb >= 1:
                            sD2.wait(vector, sD20 + qb)
                        if ST("vrows"):
                            i = vector.engine_nop()
                        else:
                            vector.tensor_scalar_mul(rows[:, 0:Q], ps_l[0][:, :], inv_d)
                            vector.tensor_scalar_mul(rows[:, Q:2 * Q], ps_l[1][:, :], inv_d)
                            vector.tensor_mul(rows[:, 3 * Q:4 * Q], rows[:, 0:Q], rows[:, 0:Q])
                            vector.tensor_sub(rows[:, 2 * Q:3 * Q], rows[:, Q:2 * Q],
                                              rows[:, 3 * Q:4 * Q])
                            i = vector.tensor_scalar_add(rows[:, 2 * Q:3 * Q],
                                                         rows[:, 2 * Q:3 * Q], c.eps)
                        vC6.inc(vector.drain(fusable=False))
                        wkey(vector, ("ln2_bc", qb))
                        for d in range(c.ND):
                            wkey(vector, ("a_x", qb, d))
                            gi = qb * c.ND + d
                            if gi >= 2:
                                wkey(vector, ("a_x2", (gi - 2) // c.ND, (gi - 2) % c.ND))
                            if ST("vapply"):
                                i = vector.engine_nop()
                            else:
                                vector.tensor_sub(wo_sb[:, d * Q:(d + 1) * Q],
                                                  wo_sb[:, d * Q:(d + 1) * Q], ps_sc[0][:, :])
                                vector.tensor_tensor(wo_sb[:, d * Q:(d + 1) * Q],
                                                     wo_sb[:, d * Q:(d + 1) * Q],
                                                     ps_sc[1][:, :], ALU.mult)
                                i = vector.tensor_add(x2st[:, (d % 2) * Q:(d % 2) * Q + Q],
                                                      wo_sb[:, d * Q:(d + 1) * Q],
                                                      xst[:, (d % 2) * Q:(d % 2) * Q + Q])
                            vC7.inc(vector.drain(fusable=False))

        def emit_ffn():
            join_all = make_join(phase_snap())
            T = c.TBLK
            NTB = c.NTB
            with ExitStack() as px:
                fcanary = px.enter_context(nc.sbuf_tensor("f_canary", [128, 16], BF16))
                xn3 = px.enter_context(nc.sbuf_tensor("f_x", [128, c.ND * T], BF16))
                wgb = px.enter_context(nc.sbuf_tensor("f_wg", [128, 2 * c.ND * 128], BF16))
                wub = px.enter_context(nc.sbuf_tensor("f_wu", [128, 2 * c.ND * 128], BF16))
                wdb = px.enter_context(nc.sbuf_tensor("f_wd", [128, 2 * c.NF * 128], BF16))
                silu_s = px.enter_context(nc.sbuf_tensor("f_si", [128, 2 * T], F32))
                mid = px.enter_context(nc.sbuf_tensor("f_mid", [128, c.NF * T], BF16))
                y_s = px.enter_context(nc.sbuf_tensor("f_y", [128, c.ND * T], F32))
                y_bf = px.enter_context(nc.sbuf_tensor("f_ybf", [128, c.ND * T], BF16))
                x2t = px.enter_context(nc.sbuf_tensor("f_x2", [128, 2 * T], F32))
                ost = px.enter_context(nc.sbuf_tensor("f_o", [128, 2 * T], F32))
                rows = px.enter_context(nc.sbuf_tensor("f_rows", [1, 4 * T], F32))

                ps_g = [px.enter_context(nc.psum_tensor(f"f_g{i}", [128, T], F32))
                        for i in range(2)]
                ps_u = [px.enter_context(nc.psum_tensor(f"f_u{i}", [128, T], F32))
                        for i in range(2)]
                ps_dn = [px.enter_context(nc.psum_tensor(f"f_d{i}", [128, T], F32))
                         for i in range(2)]
                ps_st = [px.enter_context(nc.psum_tensor(f"f_s{i}", [1, T], F32))
                         for i in range(2)]

                tA0, sB0, sD10, sD20 = tA.n, sB.n, sD1.n, sD2.n
                vC10, vC20, vC30, vC40, vC70 = vC1.n, vC2.n, vC3.n, vC4.n, vC7.n

                def tA_up(tb, hc):
                    return tA0 + tb * (c.NF + c.ND) + hc + 1

                def tA_dn(tb, dc):
                    return tA0 + tb * (c.NF + c.ND) + c.NF + dc + 1

                @block.gpsimd
                def _(gpsimd):
                    join_all(gpsimd)
                    for tb in range(NTB):
                        for hc in range(c.NF):
                            gi = tb * c.NF + hc
                            if gi >= 2:
                                ptb, phc = (gi - 2) // c.NF, (gi - 2) % c.NF
                                tA.wait(gpsimd, tA_up(ptb, phc))
                            i = gpsimd.dma_start(
                                out=bass.AP(wgb, (hc % 2) * c.ND * 128,
                                            [[2 * c.ND * 128, 128], [128, c.ND], [1, 128]]),
                                in_=bass.AP(wg_h, hc * c.D * 128,
                                            [[c.ND * 128, 128], [128, c.ND], [1, 128]]))
                            dg.inc(i)
                            i = gpsimd.dma_start(
                                out=bass.AP(wub, (hc % 2) * c.ND * 128,
                                            [[2 * c.ND * 128, 128], [128, c.ND], [1, 128]]),
                                in_=bass.AP(wu_h, hc * c.D * 128,
                                            [[c.ND * 128, 128], [128, c.ND], [1, 128]]))
                            dg.inc(i)
                            mark(("f_w", gi), dg)
                        for dc in range(c.ND):
                            gi = tb * c.ND + dc
                            if gi >= 2:
                                ptb, pdc = (gi - 2) // c.ND, (gi - 2) % c.ND
                                tA.wait(gpsimd, tA_dn(ptb, pdc))
                            i = gpsimd.dma_start(
                                out=bass.AP(wdb, (dc % 2) * c.NF * 128,
                                            [[2 * c.NF * 128, 128], [128, c.NF], [1, 128]]),
                                in_=bass.AP(wd_h, dc * c.F * 128,
                                            [[c.NF * 128, 128], [128, c.NF], [1, 128]]))
                            dg.inc(i)
                            mark(("f_wd", gi), dg)
                        i = gpsimd.dma_start(
                            out=fcanary[:, :],
                            in_=bass.AP(wd_h, (c.ND - 1) * c.F * 128,
                                        [[c.NF * 128, 128], [1, 16]]))
                        dg.inc(i)
                        mark(("f_can", tb), dg)

                @block.sync
                def _(sync):
                    join_all(sync)
                    for tb in range(NTB):
                        if tb >= 1:
                            tA.wait(sync, tA_up(tb - 1, c.NF - 1))
                        i = sync.dma_start(
                            out=bass.AP(xn3, 0, [[c.ND * T, 128], [T, c.ND], [1, T]]),
                            in_=bass.AP(xn3_sc, tb * T,
                                        [[c.S, 128], [128 * c.S, c.ND], [1, T]]))
                        dl.inc(i)
                        mark(("f_xn3", tb), dl)
                        for d in range(c.ND):
                            gi = tb * c.ND + d
                            if gi >= 2:
                                vC7.wait(sync, vC70 + gi - 1)
                            i = sync.dma_start(out=x2t[:, (d % 2) * T:(d % 2) * T + T],
                                               in_=x2_sc[d, :, tb * T:(tb + 1) * T])
                            dl.inc(i)
                            mark(("f_x2", tb, d), dl)
                            vC7.wait(sync, vC70 + gi + 1)
                            i = sync.dma_start(out=out_h[d, :, tb * T:(tb + 1) * T],
                                               in_=ost[:, (d % 2) * T:(d % 2) * T + T])
                            dl.inc(i)
                            mark(("f_out", tb, d), dl)
                    dl.wait(sync)

                @block.tensor
                def _(tensor):
                    join_all(tensor)
                    for tb in range(NTB):
                        wkey(tensor, ("f_xn3", tb))
                        for hc in range(c.NF):
                            gi = tb * c.NF + hc
                            if hc + 1 < c.NF:
                                wkey(tensor, ("f_w", gi + 1))
                            else:
                                wkey(tensor, ("f_wd", tb * c.ND))
                            if gi >= 2:
                                sB.wait(tensor, sB0 + gi - 1)
                                vC1.wait(tensor, vC10 + gi - 1)
                            if tb >= 1 and hc < 2:
                                vC7.wait(tensor, vC70 + tb * c.ND)
                            if tb >= 1 and hc == 0:
                                pass
                            for d in range(c.ND):
                                i = tensor.matmul(
                                    ps_g[hc % 2][:, :],
                                    wgb[:, (hc % 2) * c.ND * 128 + d * 128:
                                        (hc % 2) * c.ND * 128 + (d + 1) * 128],
                                    xn3[:, d * T:(d + 1) * T],
                                    start=(d == 0), stop=(d == c.ND - 1))
                            st.inc(tensor.drain(fusable=False))
                            mark(("f_g", gi), st)
                            for d in range(c.ND):
                                i = tensor.matmul(
                                    ps_u[hc % 2][:, :],
                                    wub[:, (hc % 2) * c.ND * 128 + d * 128:
                                        (hc % 2) * c.ND * 128 + (d + 1) * 128],
                                    xn3[:, d * T:(d + 1) * T],
                                    start=(d == 0), stop=(d == c.ND - 1))
                            st.inc(tensor.drain(fusable=False))
                            mark(("f_u", gi), st)
                            tA.inc(tensor.drain(fusable=False))
                        vC1.wait(tensor, vC10 + (tb + 1) * c.NF)
                        for dc in range(c.ND):
                            gi = tb * c.ND + dc
                            if dc + 1 < c.ND:
                                wkey(tensor, ("f_wd", gi + 1))
                            else:
                                wkey(tensor, ("f_can", tb))
                            if dc >= 2:
                                vC2.wait(tensor, vC20 + tb * c.ND + dc - 1)
                                sD1.wait(tensor, sD10 + tb * c.ND + dc - 1)
                            for hc in range(c.NF):
                                i = tensor.matmul(
                                    ps_dn[dc % 2][:, :],
                                    wdb[:, (dc % 2) * c.NF * 128 + hc * 128:
                                        (dc % 2) * c.NF * 128 + (hc + 1) * 128],
                                    mid[:, hc * T:(hc + 1) * T],
                                    start=(hc == 0), stop=(hc == c.NF - 1))
                            st.inc(tensor.drain(fusable=False))
                            mark(("f_dn", gi), st)
                            tA.inc(tensor.drain(fusable=False))
                        sD1.wait(tensor, sD10 + (tb + 1) * c.ND)
                        if tb >= 1:
                            vC4.wait(tensor, vC40 + tb)
                        for d in range(c.ND):
                            i = tensor.matmul(ps_st[0][:, :], ones_col[:, :],
                                              y_bf[:, d * T:(d + 1) * T],
                                              start=(d == 0), stop=(d == c.ND - 1))
                        st.inc(tensor.drain(fusable=False))
                        mark(("f_s0", tb), st)
                        vC3.wait(tensor, vC30 + tb + 1)
                        for d in range(c.ND):
                            i = tensor.matmul(ps_st[1][:, :], ones_col[:, :],
                                              mid[:, d * T:(d + 1) * T],
                                              start=(d == 0), stop=(d == c.ND - 1))
                        st.inc(tensor.drain(fusable=False))
                        mark(("f_s1", tb), st)
                        vC4.wait(tensor, vC40 + tb + 1)
                        sD2.wait(tensor, sD20 + tb + 1)
                        i = tensor.matmul(ps_g[0][:, :], ones_row[:, :], rows[:, 0:T],
                                          start=True, stop=True)
                        st.inc(tensor.drain(fusable=False))
                        i = tensor.matmul(ps_g[1][:, :], ones_row[:, :],
                                          rows[:, 2 * T:3 * T], start=True, stop=True)
                        st.inc(tensor.drain(fusable=False))
                        mark(("f_bc", tb), st)

                @block.scalar
                def _(scalar):
                    join_all(scalar)
                    for tb in range(NTB):
                        for hc in range(c.NF):
                            gi = tb * c.NF + hc
                            wkey(scalar, ("f_g", gi))
                            if gi >= 2:
                                vC1.wait(scalar, vC10 + gi - 1)
                            i = scalar.activation(silu_s[:, (hc % 2) * T:(hc % 2) * T + T],
                                                  ps_g[hc % 2][:, :], AF.Silu)
                            sB.inc(scalar.drain(fusable=False))
                        for d in range(c.ND):
                            vC2.wait(scalar, vC20 + tb * c.ND + d + 1)
                            i = scalar.copy(y_bf[:, d * T:(d + 1) * T],
                                            y_s[:, d * T:(d + 1) * T])
                            sD1.inc(scalar.drain(fusable=False))
                        vC4.wait(scalar, vC40 + tb + 1)
                        scalar.activation(rows[:, 3 * T:4 * T],
                                          rows[:, 2 * T:3 * T], AF.Ln)
                        scalar.drain(fusable=False)
                        i = scalar.activation(rows[:, 2 * T:3 * T], rows[:, 3 * T:4 * T],
                                              AF.Exp, scale=-0.5)
                        sD2.inc(scalar.drain(fusable=False))

                @block.vector
                def _(vector):
                    join_all(vector)
                    for tb in range(NTB):
                        for hc in range(c.NF):
                            gi = tb * c.NF + hc
                            sB.wait(vector, sB0 + gi + 1)
                            wkey(vector, ("f_u", gi))
                            if tb >= 1 and hc < c.ND:
                                wkey(vector, ("f_s1", tb - 1))
                            i = vector.tensor_tensor(mid[:, hc * T:(hc + 1) * T],
                                                     ps_u[hc % 2][:, :],
                                                     silu_s[:, (hc % 2) * T:(hc % 2) * T + T],
                                                     ALU.mult)
                            vC1.inc(vector.drain(fusable=False))
                        for d in range(c.ND):
                            wkey(vector, ("f_dn", tb * c.ND + d))
                            i = vector.tensor_copy(y_s[:, d * T:(d + 1) * T],
                                                   ps_dn[d % 2][:, :])
                            vC2.inc(vector.drain(fusable=False))
                        wkey(vector, ("f_dn", tb * c.ND + c.ND - 1))
                        for d in range(c.ND):
                            sD1.wait(vector, sD10 + tb * c.ND + d + 1)
                            i = vector.tensor_mul(mid[:, d * T:(d + 1) * T],
                                                  y_bf[:, d * T:(d + 1) * T],
                                                  y_bf[:, d * T:(d + 1) * T])
                        vC3.inc(vector.drain(fusable=False))
                        wkey(vector, ("f_s1", tb))
                        if tb >= 1:
                            sD2.wait(vector, sD20 + tb)
                        vector.tensor_scalar_mul(rows[:, 0:T], ps_st[0][:, :], inv_d)
                        vector.tensor_scalar_mul(rows[:, T:2 * T], ps_st[1][:, :], inv_d)
                        vector.tensor_mul(rows[:, 3 * T:4 * T], rows[:, 0:T], rows[:, 0:T])
                        vector.tensor_sub(rows[:, 2 * T:3 * T], rows[:, T:2 * T],
                                          rows[:, 3 * T:4 * T])
                        i = vector.tensor_scalar_add(rows[:, 2 * T:3 * T],
                                                     rows[:, 2 * T:3 * T], c.eps)
                        vC4.inc(vector.drain(fusable=False))
                        wkey(vector, ("f_bc", tb))
                        for d in range(c.ND):
                            wkey(vector, ("f_x2", tb, d))
                            gi = tb * c.ND + d
                            if gi >= 2:
                                wkey(vector, ("f_out", (gi - 2) // c.ND, (gi - 2) % c.ND))
                            vector.tensor_sub(y_s[:, d * T:(d + 1) * T],
                                              y_s[:, d * T:(d + 1) * T], ps_g[0][:, :])
                            vector.tensor_tensor(y_s[:, d * T:(d + 1) * T],
                                                 y_s[:, d * T:(d + 1) * T],
                                                 ps_g[1][:, :], ALU.mult)
                            i = vector.tensor_add(ost[:, (d % 2) * T:(d % 2) * T + T],
                                                  y_s[:, d * T:(d + 1) * T],
                                                  x2t[:, (d % 2) * T:(d % 2) * T + T])
                            vC7.inc(vector.drain(fusable=False))

        with ExitStack() as p12:
            kcache = p12.enter_context(nc.sbuf_tensor("kcache", [128, c.HKV * c.S], BF16))
            vcache = p12.enter_context(nc.sbuf_tensor("vcache", [128, c.NT * c.KVD], BF16))
            with ExitStack() as p01:
                xn1 = p01.enter_context(nc.sbuf_tensor("xn1", [128, c.ND * c.S], BF16))
                if "ln1" in phases:
                    emit_ln("ln1", x_h, xn1, None)
                if "qkv" in phases:
                    emit_qkv(xn1, kcache, vcache)
            if "attn" in phases:
                emit_attn(kcache, vcache)
        if "ln3" in phases:
            emit_ln("ln3", x2_sc, None, xn3_sc)
        if "ffn" in phases:
            emit_ffn()
        if "ffn" not in phases:
            snap = phase_snap()

            @block.sync
            def _(sync):
                make_join(snap)(sync)
                if "attn" in phases:
                    i = sync.dma_start(out=out_h[:, :, :], in_=x2_sc[:, :, :])
                    dl.inc(i)
                elif "ln3" in phases:
                    i = sync.dma_start(out=out_h[:, :, :], in_=x2_sc[:, :, :])
                    dl.inc(i)
                if dbg_q is not None:
                    i = sync.dma_start(out=dbg_q[:, :, :], in_=q_sc[:, :, :])
                    dl.inc(i)
                if dbg_x3 is not None:
                    i = sync.dma_start(out=dbg_x3[:, :, :], in_=xn3_sc[:, :, :])
                    dl.inc(i)
                dl.wait(sync)

    return nc



def _host_constants(c: Cfg):
    i = np.arange(0, c.DH, 2, dtype=np.float64) / c.DH
    freqs = 1.0 / (c.theta ** i)
    t = np.arange(c.S, dtype=np.float64)
    ang = np.outer(freqs, t)
    cos = np.cos(ang)
    sin = np.sin(ang)
    c1 = np.concatenate([cos, cos], axis=0).astype(np.float32)
    c2 = np.concatenate([-sin, sin], axis=0).astype(np.float32)
    masks = np.zeros((c.NDIAG, 128, c.QBLK), dtype=np.float32)
    r = np.arange(128)[:, None]
    q = np.arange(c.QBLK)[None, :]
    for j in range(c.NDIAG):
        masks[j] = (128 * j + r <= q).astype(np.float32)
    return c1, c2, masks.astype(ml_dtypes.bfloat16)


def _prep_weights(c, wqkv, wo, wg, wu, wd):
    def colblocks(w, nb):
        K = w.shape[0]
        a = w.reshape(K // 128, 128, nb, 128)
        return np.ascontiguousarray(a.transpose(2, 1, 0, 3))
    wqkv_r = colblocks(wqkv, c.NQKV)
    wo_r = colblocks(wo, c.ND)
    wg_r = colblocks(wg.T, c.NF)
    wu_r = colblocks(wu.T, c.NF)
    wd_r = colblocks(wd.T, c.ND)
    return wqkv_r, wo_r, wg_r, wu_r, wd_r


def make_in_maps(c, x, wqkv, wo, wg, wu, wd):
    c1, c2, masks = _host_constants(c)
    wqkv_r, wo_r, wg_r, wu_r, wd_r = _prep_weights(c, wqkv, wo, wg, wu, wd)
    in_maps = []
    for b in range(x.shape[0]):
        xb = np.ascontiguousarray(x[b].T).reshape(c.ND, 128, c.S)
        in_maps.append({
            "x": xb, "wqkv_r": wqkv_r, "wo_r": wo_r,
            "wg_r": wg_r, "wu_r": wu_r, "wd_r": wd_r,
            "rope_c1": c1, "rope_c2": c2, "masks": masks,
        })
    return in_maps


def kernel(L0_attn_x, L0_attn_Wqkv, L0_attn_Wo, L0_ffn_w_gate, L0_ffn_w_up,
           L0_ffn_w_down):
    c = Cfg()
    x = np.asarray(L0_attn_x)
    B = x.shape[0]
    in_maps = make_in_maps(c, x, np.asarray(L0_attn_Wqkv), np.asarray(L0_attn_Wo),
                           np.asarray(L0_ffn_w_gate), np.asarray(L0_ffn_w_up),
                           np.asarray(L0_ffn_w_down))
    nc = build_core_program(c)
    res = run_bass_kernel_spmd(nc, in_maps, core_ids=list(range(B)))
    outs = []
    for b in range(B):
        ob = np.asarray(res.results[b]["out"]).reshape(c.D, c.S)
        outs.append(ob.T)
    return np.ascontiguousarray(np.stack(outs)).astype(np.float32)


# version 06 (final)
# speedup vs baseline: 1.0129x; 1.0129x over previous
import math
from contextlib import ExitStack

import numpy as np
import ml_dtypes

import concourse.bass as bass
import concourse.mybir as mybir
from concourse.bass_utils import run_bass_kernel_spmd

F32 = mybir.dt.float32
BF16 = mybir.dt.bfloat16
AF = mybir.ActivationFunctionType
ALU = mybir.AluOpType


class Cfg:
    def __init__(self, S=2048, D=2048, HQ=16, HKV=4, F=8192, TBLK=512, QBLK=512,
                 theta=1e4, eps=1e-5):
        self.S, self.D, self.HQ, self.HKV, self.F = S, D, HQ, HKV, F
        self.TBLK, self.QBLK = TBLK, QBLK
        self.theta, self.eps = theta, eps
        self.DH = 128
        self.ND = D // 128
        self.NF = F // 128
        self.NT = S // 128
        self.NTB = S // TBLK
        self.NQB = S // QBLK
        self.NDIAG = QBLK // 128
        self.G = HQ // HKV
        self.KVD = HKV * self.DH
        self.QKVW = (HQ + 2 * HKV) * self.DH
        self.NQKV = self.QKVW // 128
        assert D % 128 == 0 and F % 128 == 0 and S % TBLK == 0 and S % QBLK == 0
        assert TBLK <= 512 and QBLK <= 512 and QBLK % 128 == 0


class Ctr:
    def __init__(self, nc, ctx, name, step=1):
        self.h = ctx.enter_context(nc.semaphore(name))
        self.n = 0
        self.step = step

    def inc(self, instr, k=1):
        instr.then_inc(self.h, self.step * k)
        self.n += self.step * k
        return self.n

    def wait(self, eng, at=None):
        v = self.n if at is None else at
        if v > 0:
            eng.wait_ge(self.h, v)


def build_core_program(cfg: Cfg, phases=("ln1", "qkv", "attn", "ln3", "ffn"), stub=()):
    c = cfg
    nc = bass.Bass("TRN2", target_bir_lowering=False, detect_race_conditions=False)

    x_h = nc.dram_tensor("x", [c.ND, 128, c.S], F32, kind="ExternalInput")
    wqkv_h = nc.dram_tensor("wqkv_r", [c.NQKV, 128, c.ND, 128], F32, kind="ExternalInput")
    wo_h = nc.dram_tensor("wo_r", [c.ND, 128, c.HQ, 128], F32, kind="ExternalInput")
    wg_h = nc.dram_tensor("wg_r", [c.NF, 128, c.ND, 128], F32, kind="ExternalInput")
    wu_h = nc.dram_tensor("wu_r", [c.NF, 128, c.ND, 128], F32, kind="ExternalInput")
    wd_h = nc.dram_tensor("wd_r", [c.ND, 128, c.NF, 128], F32, kind="ExternalInput")
    c1_h = nc.dram_tensor("rope_c1", [128, c.S], F32, kind="ExternalInput")
    c2_h = nc.dram_tensor("rope_c2", [128, c.S], F32, kind="ExternalInput")
    mask_h = nc.dram_tensor("masks", [c.NDIAG, 128, c.QBLK], BF16, kind="ExternalInput")
    out_h = nc.dram_tensor("out", [c.ND, 128, c.S], F32, kind="ExternalOutput")

    q_sc = nc.dram_tensor("q_sc", [c.HQ, 128, c.S], BF16)
    wg_bf = nc.dram_tensor("wg_bf", [c.NF, 128, c.ND, 128], BF16)
    wu_bf = nc.dram_tensor("wu_bf", [c.NF, 128, c.ND, 128], BF16)
    wd_bf = nc.dram_tensor("wd_bf", [c.ND, 128, c.NF, 128], BF16)
    dbg_x2o = dbg_x3o = dbg_qo = None
    if "dumpx2" in phases:
        dbg_x2o = nc.dram_tensor("dbg_x2o", [c.ND, 128, c.S], F32, kind="ExternalOutput")
        dbg_x3o = nc.dram_tensor("dbg_x3o", [c.ND, 128, c.S], BF16, kind="ExternalOutput")
        dbg_qo = nc.dram_tensor("dbg_qo", [c.HQ, 128, c.S], BF16, kind="ExternalOutput")
    dbg_q = dbg_x3 = dbg_wb = None
    if "ffn" not in phases or "dumpp0" in stub:
        dbg_q = nc.dram_tensor("dbg_q", [c.HQ, 128, c.S], BF16, kind="ExternalOutput")
        dbg_x3 = nc.dram_tensor("dbg_x3", [c.ND, 128, c.S], BF16, kind="ExternalOutput")
        dbg_wb = nc.dram_tensor("dbg_wb", [128, c.ND * 128], BF16, kind="ExternalOutput")
        dbg_p = nc.dram_tensor("dbg_p", [128, c.G * c.NT * c.QBLK], BF16, kind="ExternalOutput")
        dbg_af = nc.dram_tensor("dbg_af", [128, c.HQ * c.QBLK], BF16, kind="ExternalOutput")
        dbg_rl = nc.dram_tensor("dbg_rl", [1, 2 * c.QBLK], BF16, kind="ExternalOutput")
        dbg_wo = nc.dram_tensor("dbg_wo", [128, c.ND * c.QBLK], F32, kind="ExternalOutput")
    x2_sc = nc.dram_tensor("x2_sc", [c.ND, 128, c.S], F32)
    xn3_sc = nc.dram_tensor("xn3_sc", [c.ND, 128, c.S], BF16)

    inv_d = 1.0 / c.D
    sc_scale = 1.0 / math.sqrt(c.DH)

    M = {}

    def mark(key, ctr):
        M[key] = (ctr, ctr.n)

    def wkey(eng, key):
        ctr, v = M[key]
        if v > 0:
            eng.wait_ge(ctr.h, v)

    ctx = ExitStack()
    with ctx:
        ones_col = ctx.enter_context(nc.sbuf_tensor("ones_col", [128, 1], BF16))
        ones_row = ctx.enter_context(nc.sbuf_tensor("ones_row", [1, 128], F32))
        ones_row_bf = ctx.enter_context(nc.sbuf_tensor("ones_row_bf", [1, 128], BF16))

        dl = Ctr(nc, ctx, "dl", 16)
        ds = Ctr(nc, ctx, "ds", 16)
        dg = Ctr(nc, ctx, "dg", 16)
        st = Ctr(nc, ctx, "st", 1)
        ss = Ctr(nc, ctx, "ss", 1)
        sv = Ctr(nc, ctx, "sv", 1)
        tA = Ctr(nc, ctx, "tA", 1)
        sB = Ctr(nc, ctx, "sB", 1)
        sD1 = Ctr(nc, ctx, "sD1", 1)
        sD2 = Ctr(nc, ctx, "sD2", 1)
        vC1 = Ctr(nc, ctx, "vC1", 1)
        vC2 = Ctr(nc, ctx, "vC2", 1)
        vC3 = Ctr(nc, ctx, "vC3", 1)
        vC4 = Ctr(nc, ctx, "vC4", 1)
        vC6 = Ctr(nc, ctx, "vC6", 1)
        vC7 = Ctr(nc, ctx, "vC7", 1)
        ALLC = [dl, ds, dg, st, ss, sv, tA, sB, sD1, sD2, vC1, vC2, vC3, vC4, vC6, vC7]

        block = ctx.enter_context(nc.Block())

        def phase_snap():
            return [(ctr, ctr.n) for ctr in ALLC]

        def make_join(snap):
            def join_all(eng):
                for ctr, v in snap:
                    ctr.wait(eng, v)
            return join_all

        @block.vector
        def _(vector):
            vector.memset(ones_col[:, :], 1.0)
            vector.memset(ones_row[:, :], 1.0)
            vector.memset(ones_row_bf[:, :], 1.0)
            sv.inc(vector.drain(fusable=False))
        mark("ones", sv)

        conv_jobs = []
        for (srch, dsth, nblk, inner) in ((wg_h, wg_bf, c.NF, c.ND),
                                          (wu_h, wu_bf, c.NF, c.ND),
                                          (wd_h, wd_bf, c.ND, c.NF)):
            step = max(1, (8 * 1024 * 1024) // (128 * inner * 128 * 4))
            for b0 in range(0, nblk, step):
                nb = min(step, nblk - b0)
                conv_jobs.append((srch, dsth, b0 * 128 * inner * 128,
                                  nb * 128 * inner * 128))
        conv_state = {"i": 0}

        def emit_conv(gpsimd, k=1):
            for _ in range(k):
                if conv_state["i"] >= len(conv_jobs):
                    return
                srch, dsth, off, n = conv_jobs[conv_state["i"]]
                conv_state["i"] += 1
                i = gpsimd.dma_start(out=bass.AP(dsth, off, [[1, 1], [1, n]]),
                                     in_=bass.AP(srch, off, [[1, 1], [1, n]]))
                dg.inc(i)

        def emit_ln(tag, src_hbm, xn1_dst, dst_hbm):
            T = c.TBLK
            NTB = c.NTB
            resident = xn1_dst is not None
            join_all = make_join(phase_snap())
            with ExitStack() as px:
                x_s = px.enter_context(nc.sbuf_tensor(f"{tag}_x", [128, c.ND * T], F32))
                x_bf = px.enter_context(nc.sbuf_tensor(f"{tag}_xbf", [128, c.ND * T], BF16))
                x_sq = px.enter_context(nc.sbuf_tensor(f"{tag}_xsq", [128, c.ND * T], BF16))
                rows = px.enter_context(nc.sbuf_tensor(f"{tag}_rows", [1, 4 * T], F32))
                ost = (None if resident else
                       px.enter_context(nc.sbuf_tensor(f"{tag}_o", [128, c.ND * T], BF16)))
                ps_st0 = px.enter_context(nc.psum_tensor(f"{tag}_s0", [1, T], F32))
                ps_st1 = px.enter_context(nc.psum_tensor(f"{tag}_s1", [1, T], F32))
                ps_bc1 = px.enter_context(nc.psum_tensor(f"{tag}_b1", [128, T], F32))
                ps_bc2 = px.enter_context(nc.psum_tensor(f"{tag}_b2", [128, T], F32))

                vA0, sB0, vC60, sD20, vC30, vC70 = (vC1.n, sB.n, vC6.n, sD2.n, vC3.n, vC7.n)

                @block.sync
                def _(sync):
                    join_all(sync)
                    for tb in range(NTB):
                        if not resident and tb >= 1:
                            for d in range(c.ND):
                                vC7.wait(sync, vC70 + (tb - 1) * c.ND + d + 1)
                                i = sync.dma_start(
                                    out=dst_hbm[d, :, (tb - 1) * T: tb * T],
                                    in_=ost[:, d * T:(d + 1) * T])
                                dl.inc(i)
                                mark((tag, "store", tb - 1, d), dl)
                        vC1.wait(sync, vA0 + tb)
                        i = sync.dma_start(
                            out=bass.AP(x_s, 0, [[c.ND * T, 128], [T, c.ND], [1, T]]),
                            in_=bass.AP(src_hbm, tb * T,
                                        [[c.S, 128], [128 * c.S, c.ND], [1, T]]))
                        dl.inc(i)
                        mark((tag, "load", tb), dl)
                    if not resident:
                        for d in range(c.ND):
                            vC7.wait(sync, vC70 + (NTB - 1) * c.ND + d + 1)
                            i = sync.dma_start(
                                out=dst_hbm[d, :, (NTB - 1) * T: NTB * T],
                                in_=ost[:, d * T:(d + 1) * T])
                            dl.inc(i)
                            mark((tag, "store", NTB - 1, d), dl)

                @block.tensor
                def _(tensor):
                    join_all(tensor)
                    for tb in range(NTB):
                        sB.wait(tensor, sB0 + tb + 1)
                        if tb >= 1:
                            vC6.wait(tensor, vC60 + tb)
                            vC1.wait(tensor, vA0 + tb)
                        for d in range(c.ND):
                            i = tensor.matmul(ps_st0[:, :], ones_col[:, :],
                                              x_bf[:, d * T:(d + 1) * T],
                                              start=(d == 0), stop=(d == c.ND - 1))
                        st.inc(tensor.drain(fusable=False))
                        mark((tag, "st0", tb), st)
                        vC3.wait(tensor, vC30 + tb + 1)
                        for d in range(c.ND):
                            i = tensor.matmul(ps_st1[:, :], ones_col[:, :],
                                              x_sq[:, d * T:(d + 1) * T],
                                              start=(d == 0), stop=(d == c.ND - 1))
                        st.inc(tensor.drain(fusable=False))
                        mark((tag, "st1", tb), st)
                        vC6.wait(tensor, vC60 + tb + 1)
                        sD2.wait(tensor, sD20 + tb + 1)
                        i = tensor.matmul(ps_bc1[:, :], ones_row[:, :], rows[:, 0:T],
                                          start=True, stop=True)
                        st.inc(tensor.drain(fusable=False))
                        i = tensor.matmul(ps_bc2[:, :], ones_row[:, :],
                                          rows[:, 2 * T:3 * T], start=True, stop=True)
                        st.inc(tensor.drain(fusable=False))
                        mark((tag, "bc", tb), st)

                @block.scalar
                def _(scalar):
                    join_all(scalar)
                    for tb in range(NTB):
                        wkey(scalar, (tag, "load", tb))
                        if tb >= 1:
                            wkey(scalar, (tag, "st0", tb - 1))
                            vC3.wait(scalar, vC30 + tb)
                        for d in range(c.ND):
                            i = scalar.copy(x_bf[:, d * T:(d + 1) * T],
                                            x_s[:, d * T:(d + 1) * T])
                        sB.inc(scalar.drain(fusable=False))
                        vC6.wait(scalar, vC60 + tb + 1)
                        scalar.activation(rows[:, 3 * T:4 * T],
                                          rows[:, 2 * T:3 * T], AF.Ln)
                        scalar.drain(fusable=False)
                        i = scalar.activation(rows[:, 2 * T:3 * T], rows[:, 3 * T:4 * T],
                                              AF.Exp, scale=-0.5)
                        sD2.inc(scalar.drain(fusable=False))

                @block.vector
                def _(vector):
                    join_all(vector)
                    for tb in range(NTB):
                        sB.wait(vector, sB0 + tb + 1)
                        if tb >= 1:
                            wkey(vector, (tag, "st1", tb - 1))
                        for d in range(c.ND):
                            i = vector.tensor_mul(x_sq[:, d * T:(d + 1) * T],
                                                  x_bf[:, d * T:(d + 1) * T],
                                                  x_bf[:, d * T:(d + 1) * T])
                        vC3.inc(vector.drain(fusable=False))
                        wkey(vector, (tag, "st1", tb))
                        if tb >= 1:
                            sD2.wait(vector, sD20 + tb)
                        vector.tensor_scalar_mul(rows[:, 0:T], ps_st0[:, :], inv_d)
                        vector.tensor_scalar_mul(rows[:, T:2 * T], ps_st1[:, :], inv_d)
                        vector.drain(fusable=False)
                        vector.tensor_mul(rows[:, 3 * T:4 * T], rows[:, 0:T], rows[:, 0:T])
                        vector.drain(fusable=False)
                        vector.tensor_sub(rows[:, 2 * T:3 * T], rows[:, T:2 * T],
                                          rows[:, 3 * T:4 * T])
                        vector.drain(fusable=False)
                        i = vector.tensor_scalar_add(rows[:, 2 * T:3 * T],
                                                     rows[:, 2 * T:3 * T], c.eps)
                        vC6.inc(vector.drain(fusable=False))
                        wkey(vector, (tag, "bc", tb))
                        for d in range(c.ND):
                            vector.tensor_sub(x_s[:, d * T:(d + 1) * T],
                                              x_s[:, d * T:(d + 1) * T], ps_bc1[:, :])
                            if resident:
                                dst = xn1_dst[:, d * c.S + tb * T: d * c.S + tb * T + T]
                            else:
                                dst = ost[:, d * T:(d + 1) * T]
                                if tb >= 1:
                                    wkey(vector, (tag, "store", tb - 1, d))
                            i = vector.tensor_tensor(dst, x_s[:, d * T:(d + 1) * T],
                                                     ps_bc2[:, :], ALU.mult)
                            if not resident:
                                vC7.inc(vector.drain(fusable=False))
                        vC1.inc(vector.drain(fusable=False))

        def emit_qkv(xn1, kcache, vcache):
            join_all = make_join(phase_snap())
            def ST(name):
                return name in stub
            TB = min(512, c.S)
            NB = c.S // TB
            NQK = c.HQ + c.HKV
            forder = list(range(NQK))[::-1] if ST("revf") else list(range(NQK))
            qpos = [fi * NB + tb for fi, f in enumerate(forder)
                    for tb in range(NB) if f < c.HQ]
            qprev = {idx: (qpos[i - 2] if i >= 2 else None)
                     for i, idx in enumerate(qpos)}
            with ExitStack() as px:
                wbuf = px.enter_context(nc.sbuf_tensor("q_w", [128, 2 * c.ND * 128], BF16))
                vwbuf = px.enter_context(nc.sbuf_tensor("q_vw", [128, c.ND * c.KVD], BF16))
                p_s = px.enter_context(nc.sbuf_tensor("q_ps", [128, 2 * TB], F32))
                p_w = px.enter_context(nc.sbuf_tensor("q_pw", [128, 2 * TB], F32))
                m1 = px.enter_context(nc.sbuf_tensor("q_m1", [128, 2 * TB], F32))
                qstage = px.enter_context(nc.sbuf_tensor("q_qs", [128, 2 * TB], BF16))
                c1s = px.enter_context(nc.sbuf_tensor("q_c1", [128, c.S], F32))
                c2s = px.enter_context(nc.sbuf_tensor("q_c2", [128, c.S], F32))
                ps_qk = [px.enter_context(nc.psum_tensor(f"q_p{i}", [128, TB], F32))
                         for i in range(2)]
                ps_v = [px.enter_context(nc.psum_tensor(f"q_v{i}", [128, c.KVD], F32))
                        for i in range(2)]

                tA0, sB0, vC10, vC20 = tA.n, sB.n, vC1.n, vC2.n

                @block.gpsimd
                def _(gpsimd):
                    join_all(gpsimd)
                    for fi, f in enumerate(forder):
                        if fi >= 2:
                            tA.wait(gpsimd, tA0 + fi - 1)
                        i = gpsimd.dma_start(
                            out=bass.AP(wbuf, (fi % 2) * c.ND * 128,
                                        [[2 * c.ND * 128, 128], [128, c.ND], [1, 128]]),
                            in_=bass.AP(wqkv_h, f * c.D * 128,
                                        [[c.ND * 128, 128], [128, c.ND], [1, 128]]))
                        dg.inc(i)
                        mark(("qkv_w", fi), dg)
                        emit_conv(gpsimd, 1)
                    for b in range(c.HKV):
                        i = gpsimd.dma_start(
                            out=bass.AP(vwbuf, b * 128,
                                        [[c.ND * c.KVD, 128], [c.KVD, c.ND], [1, 128]]),
                            in_=bass.AP(wqkv_h, (NQK + b) * c.D * 128,
                                        [[c.ND * 128, 128], [128, c.ND], [1, 128]]))
                        dg.inc(i)
                    mark(("qkv_vw",), dg)

                @block.sync
                def _(sync):
                    join_all(sync)
                    i = sync.dma_start(out=c1s[:, :], in_=c1_h[:, :])
                    dl.inc(i)
                    i = sync.dma_start(out=c2s[:, :], in_=c2_h[:, :])
                    dl.inc(i)
                    mark(("rope_tabs",), dl)
                    for fi, f in enumerate(forder):
                        for tb in range(NB):
                            idx = fi * NB + tb
                            sl = slice((idx % 2) * TB, (idx % 2) * TB + TB)
                            sB.wait(sync, sB0 + idx + 1)
                            if idx >= 2:
                                vC1.wait(sync, vC10 + idx - 1)
                            i = sync.dma_start(out=p_w[0:64, sl], in_=p_s[64:128, sl])
                            dl.inc(i)
                            i = sync.dma_start(out=p_w[64:128, sl], in_=p_s[0:64, sl])
                            dl.inc(i)
                            mark(("swap", idx), dl)
                            if f < c.HQ:
                                vC1.wait(sync, vC10 + idx + 1)
                                i = sync.dma_start(
                                    out=q_sc[f, :, tb * TB:(tb + 1) * TB],
                                    in_=qstage[:, sl])
                                dl.inc(i)
                                mark(("qstore", idx), dl)

                @block.tensor
                def _(tensor):
                    join_all(tensor)
                    for fi, f in enumerate(forder):
                        wkey(tensor, ("qkv_w", fi + 1) if fi + 1 < NQK else ("qkv_vw",))
                        if ST("delay") and fi == 0:
                            tensor.nop(cycle_cnt=50000)
                        for tb in range(NB):
                            idx = fi * NB + tb
                            if idx >= 2:
                                sB.wait(tensor, sB0 + idx - 1)
                            for d in range(c.ND):
                                i = tensor.matmul(
                                    ps_qk[idx % 2][:, :],
                                    wbuf[:, (fi % 2) * c.ND * 128 + d * 128:
                                         (fi % 2) * c.ND * 128 + (d + 1) * 128],
                                    xn1[:, d * c.S + tb * TB: d * c.S + tb * TB + TB],
                                    start=(d == 0), stop=(d == c.ND - 1))
                            st.inc(tensor.drain(fusable=False))
                            mark(("qk_mm", idx), st)
                        tA.inc(tensor.drain(fusable=False))
                    wkey(tensor, ("qkv_vw",))
                    for ts in range(c.NT):
                        if ts >= 2:
                            vC2.wait(tensor, vC20 + ts - 1)
                        for d in range(c.ND):
                            i = tensor.matmul(
                                ps_v[ts % 2][:, :],
                                xn1[:, d * c.S + ts * 128: d * c.S + ts * 128 + 128],
                                vwbuf[:, d * c.KVD:(d + 1) * c.KVD],
                                start=(d == 0), stop=(d == c.ND - 1))
                        st.inc(tensor.drain(fusable=False))
                        mark(("v_mm", ts), st)

                @block.scalar
                def _(scalar):
                    join_all(scalar)
                    for fi, f in enumerate(forder):
                        for tb in range(NB):
                            idx = fi * NB + tb
                            sl = slice((idx % 2) * TB, (idx % 2) * TB + TB)
                            wkey(scalar, ("qk_mm", idx))
                            if idx >= 2:
                                vC1.wait(scalar, vC10 + idx - 1)
                            i = scalar.copy(p_s[:, sl], ps_qk[idx % 2][:, :])
                            sB.inc(scalar.drain(fusable=False))

                @block.vector
                def _(vector):
                    join_all(vector)
                    wkey(vector, ("rope_tabs",))
                    for fi, f in enumerate(forder):
                        for tb in range(NB):
                            idx = fi * NB + tb
                            sl = slice((idx % 2) * TB, (idx % 2) * TB + TB)
                            if ST("ropecopy0"):
                                wkey(vector, ("qk_mm", idx))
                            wkey(vector, ("swap", idx))
                            if f < c.HQ and qprev[idx] is not None:
                                wkey(vector, ("qstore", qprev[idx]))
                            if f < c.HQ:
                                dst = qstage[:, sl]
                            else:
                                kv = f - c.HQ
                                dst = kcache[:, kv * c.S + tb * TB: kv * c.S + tb * TB + TB]
                            if ST("ropecopy0"):
                                i = vector.tensor_copy(dst, ps_qk[idx % 2][:, :])
                            elif ST("ropecopy1"):
                                i = vector.tensor_copy(dst, p_s[:, sl])
                            elif ST("ropecopy2"):
                                i = vector.tensor_copy(dst, p_w[:, sl])
                            else:
                                vector.tensor_mul(m1[:, sl], p_s[:, sl],
                                                  c1s[:, tb * TB:(tb + 1) * TB])
                                vector.tensor_mul(p_w[:, sl], p_w[:, sl],
                                                  c2s[:, tb * TB:(tb + 1) * TB])
                                i = vector.tensor_add(dst, m1[:, sl], p_w[:, sl])
                            vC1.inc(vector.drain(fusable=False))
                    for ts in range(c.NT):
                        wkey(vector, ("v_mm", ts))
                        i = vector.tensor_copy(vcache[:, ts * c.KVD:(ts + 1) * c.KVD],
                                               ps_v[ts % 2][:, :])
                        vC2.inc(vector.drain(fusable=False))

        def emit_attn(kcache, vcache):
            join_all = make_join(phase_snap())
            def ST(name):
                return name in stub
            Q = c.QBLK
            with ExitStack() as px:
                qslab = px.enter_context(nc.sbuf_tensor("a_q", [128, c.HQ * Q], BF16))
                probs = px.enter_context(
                    nc.sbuf_tensor("a_p", [128, c.G * c.NT * Q], BF16))
                attnf = px.enter_context(nc.sbuf_tensor("a_af", [128, c.HQ * Q], BF16))
                masks = px.enter_context(
                    nc.sbuf_tensor("a_mk", [128, c.NDIAG * Q], BF16))
                lrow = px.enter_context(nc.sbuf_tensor("a_l", [1, Q], F32))
                rlbf = px.enter_context(nc.sbuf_tensor("a_rl", [1, 2 * Q], BF16))
                rlbc = px.enter_context(nc.sbuf_tensor("a_rlbc", [128, 2 * Q], BF16))
                wobuf = px.enter_context(
                    nc.sbuf_tensor("a_wo", [128, 2 * c.HQ * 128], BF16))
                wo_sb = px.enter_context(nc.sbuf_tensor("a_wosb", [128, c.ND * Q], F32))
                canary = px.enter_context(nc.sbuf_tensor("a_can", [128, 16], BF16))
                mscr = px.enter_context(
                    nc.sbuf_tensor("a_ms", [128, 2 * c.NDIAG * Q], BF16))
                xst = px.enter_context(nc.sbuf_tensor("a_x", [128, 2 * Q], F32))
                x2st = px.enter_context(nc.sbuf_tensor("a_x2", [128, 2 * Q], F32))
                rows = px.enter_context(nc.sbuf_tensor("a_rows", [1, 4 * Q], F32))

                ps_sc = [px.enter_context(nc.psum_tensor(f"a_sc{i}", [128, Q], F32))
                         for i in range(2)]
                ps_l = [px.enter_context(nc.psum_tensor(f"a_l{i}", [1, Q], F32))
                        for i in range(2)]
                ps_pv = px.enter_context(nc.psum_tensor("a_pv", [128, Q], F32))
                ps_bc = px.enter_context(nc.psum_tensor("a_bc", [128, Q], F32))
                ps_wo = [px.enter_context(nc.psum_tensor(f"a_wo{i}", [128, Q], F32))
                         for i in range(2)]
                assert 2 * c.ND <= c.G * c.NT, "probs scratch too small for wo_bf"
                wo_bf = probs[:, c.ND * Q: 2 * c.ND * Q]

                tA0, sB0, sD10, sD20 = tA.n, sB.n, sD1.n, sD2.n
                vC10, vC20, vC30, vC40, vC60, vC70 = (vC1.n, vC2.n, vC3.n,
                                                      vC4.n, vC6.n, vC7.n)

                def kcnt(qb):
                    return (qb + 1) * c.NDIAG

                def exps_before(qb):
                    return c.HQ * sum(kcnt(i) for i in range(qb))

                def vC1_base(qb):
                    return vC10 + sum(c.HQ * c.NDIAG + c.ND for _ in range(qb)) * 0 + \
                        qb * (c.HQ * c.NDIAG + c.ND)

                @block.gpsimd
                def _(gpsimd):
                    join_all(gpsimd)
                    for qb in range(c.NQB):
                        for d in range(c.ND):
                            gidx = qb * c.ND + d
                            if gidx >= 2:
                                tA.wait(gpsimd, tA0 + gidx - 1)
                            if ST("dwo"):
                                i = gpsimd.dma_start(
                                    out=wobuf[:, 0:128],
                                    in_=bass.AP(wo_h, 0, [[c.HQ * 128, 128], [1, 128]]))
                            else:
                                i = gpsimd.dma_start(
                                    out=bass.AP(wobuf, (d % 2) * c.HQ * 128,
                                                [[2 * c.HQ * 128, 128], [128, c.HQ], [1, 128]]),
                                    in_=bass.AP(wo_h, d * c.HQ * c.DH * 128,
                                                [[c.HQ * 128, 128], [128, c.HQ], [1, 128]]))
                            dg.inc(i)
                            mark(("wo_w", gidx), dg)
                            emit_conv(gpsimd, 1)
                    emit_conv(gpsimd, len(conv_jobs))
                    i = gpsimd.dma_start(
                        out=canary[:, :],
                        in_=bass.AP(wo_h, (c.ND - 1) * c.HQ * c.DH * 128,
                                    [[c.HQ * 128, 128], [1, 16]]))
                    dg.inc(i)
                    mark(("wo_can",), dg)

                @block.sync
                def _(sync):
                    join_all(sync)
                    if ST("dmask"):
                        i = sync.dma_start(out=masks[:, 0:Q],
                                           in_=bass.AP(mask_h, 0, [[Q, 128], [1, Q]]))
                    else:
                        i = sync.dma_start(
                            out=bass.AP(masks, 0,
                                        [[c.NDIAG * Q, 128], [Q, c.NDIAG], [1, Q]]),
                            in_=bass.AP(mask_h, 0,
                                        [[Q, 128], [128 * Q, c.NDIAG], [1, Q]]))
                    dl.inc(i)
                    mark(("masks",), dl)
                    for qb in range(c.NQB):
                        if qb >= 1:
                            tA.wait(sync, tA0 + (qb - 1) * c.ND + 1)
                        if ST("dqslab"):
                            i = sync.dma_start(out=qslab[:, 0:Q],
                                               in_=bass.AP(q_sc, 0, [[c.S, 128], [1, Q]]))
                        else:
                            i = sync.dma_start(
                                out=bass.AP(qslab, 0, [[c.HQ * Q, 128], [Q, c.HQ], [1, Q]]),
                                in_=bass.AP(q_sc, qb * Q,
                                            [[c.S, 128], [128 * c.S, c.HQ], [1, Q]]))
                        dl.inc(i)
                        i = sync.dma_start(out=canary[:, :],
                                           in_=bass.AP(q_sc, qb * c.QBLK,
                                                       [[c.S, 128], [1, 16]]))
                        dl.inc(i)
                        mark(("qslab", qb), dl)
                        if ST("dumpp") and qb == c.NQB - 1:
                            vC3.wait(sync, vC30 + c.NQB * c.HQ)
                            i = sync.dma_start(out=dbg_p[:, :], in_=probs[:, :])
                            dl.inc(i)
                            i = sync.dma_start(out=dbg_af[:, :], in_=attnf[:, :])
                            dl.inc(i)
                            i = sync.dma_start(out=dbg_rl[:, :], in_=rlbf[:, :])
                            dl.inc(i)
                            mark(("pdump",), dl)
                        for d in range(c.ND):
                            gi = qb * c.ND + d
                            if gi >= 2:
                                vC7.wait(sync, vC70 + gi - 1)
                            if ST("dx"):
                                i = sync.dma_start(out=xst[:, 0:Q],
                                                   in_=x_h[0, :, 0:Q])
                            else:
                                i = sync.dma_start(out=xst[:, (d % 2) * Q:(d % 2) * Q + Q],
                                                   in_=x_h[d, :, qb * Q:(qb + 1) * Q])
                            dl.inc(i)
                            mark(("a_x", qb, d), dl)
                            vC7.wait(sync, vC70 + gi + 1)
                            if ST("dx2"):
                                i = sync.dma_start(out=x2_sc[0, :, 0:Q], in_=x2st[:, 0:Q])
                            else:
                                i = sync.dma_start(out=x2_sc[d, :, qb * Q:(qb + 1) * Q],
                                                   in_=x2st[:, (d % 2) * Q:(d % 2) * Q + Q])
                            dl.inc(i)
                            mark(("a_x2", qb, d), dl)
                            if ST("dumpp0") and qb == 0 and d == c.ND - 1:
                                for t_, s_ in ((dbg_p, probs[:, :]),
                                               (dbg_af, attnf[:, :]),
                                               (dbg_rl, rlbf[:, :]),
                                               (dbg_wo, wo_sb[:, :])):
                                    i = sync.dma_start(out=t_[:, :], in_=s_)
                                    dl.inc(i)
                                mark(("p0dump",), dl)

                @block.tensor
                def _(tensor):
                    join_all(tensor)
                    for qb in range(c.NQB):
                        kc = kcnt(qb)
                        wkey(tensor, ("qslab", qb))
                        if ST("dumpp0") and qb == 1:
                            wkey(tensor, ("p0dump",))
                        if qb >= 1:
                            vC7.wait(tensor, vC70 + qb * c.ND)
                            vC6.wait(tensor, vC60 + qb)
                        for kv in range(c.HKV):
                            for j in range(c.G):
                                h = kv * c.G + j
                                ho = qb * c.HQ + h
                                for kj in range(kc):
                                    eidx = sB0 + exps_before(qb) + h * kc + kj
                                    if kj >= 2:
                                        sB.wait(tensor, eidx - 2 + 1)
                                    if ST("sc"):
                                        i = tensor.nop()
                                    else:
                                        i = tensor.matmul(
                                            ps_sc[kj % 2][:, :],
                                            kcache[:, kv * c.S + kj * 128:
                                                   kv * c.S + (kj + 1) * 128],
                                            qslab[:, h * Q:(h + 1) * Q],
                                            start=True, stop=True)
                                    st.inc(tensor.drain(fusable=False))
                                    mark(("sc", qb, h, kj), st)
                                sB.wait(tensor, sB0 + exps_before(qb) + (h + 1) * kc)
                                vC1.wait(tensor, vC1_base(qb) + (h + 1) * c.NDIAG)
                                if h >= 2:
                                    vC2.wait(tensor, vC20 + qb * c.HQ + h - 1)
                                if ST("l"):
                                    i = tensor.nop()
                                else:
                                    for kj in range(kc):
                                        if kj >= qb * c.NDIAG:
                                            mi = (h % 2) * c.NDIAG + kj - qb * c.NDIAG
                                            rhs = mscr[:, mi * Q:(mi + 1) * Q]
                                        else:
                                            rhs = probs[:, (kj * c.G + j) * Q:
                                                        (kj * c.G + j + 1) * Q]
                                        i = tensor.matmul(
                                            ps_l[j % 2][:, :], ones_col[:, :], rhs,
                                            start=(kj == 0), stop=(kj == kc - 1))
                                st.inc(tensor.drain(fusable=False))
                                mark(("lmm", qb, h), st)
                                if h >= 1:
                                    vC3.wait(tensor, vC30 + qb * c.HQ + h)
                                if ST("pv"):
                                    i = tensor.nop()
                                else:
                                    for kj in range(kc):
                                        if kj >= qb * c.NDIAG:
                                            mi = (h % 2) * c.NDIAG + kj - qb * c.NDIAG
                                            rhs = mscr[:, mi * Q:(mi + 1) * Q]
                                        else:
                                            rhs = probs[:, (kj * c.G + j) * Q:
                                                        (kj * c.G + j + 1) * Q]
                                        i = tensor.matmul(
                                            ps_pv[:, :],
                                            vcache[:, kj * c.KVD + kv * 128:
                                                   kj * c.KVD + kv * 128 + 128],
                                            rhs,
                                            start=(kj == 0), stop=(kj == kc - 1))
                                st.inc(tensor.drain(fusable=False))
                                mark(("pv", qb, h), st)
                                vC2.wait(tensor, vC20 + qb * c.HQ + h + 1)
                                if ST("bc"):
                                    i = tensor.nop()
                                else:
                                    i = tensor.matmul(ps_bc[:, :], ones_row_bf[:, :],
                                                      rlbf[:, (j % 2) * Q:(j % 2) * Q + Q],
                                                      start=True, stop=True)
                                st.inc(tensor.drain(fusable=False))
                                mark(("lbc", qb, h), st)
                        vC3.wait(tensor, vC30 + (qb + 1) * c.HQ)
                        for d in range(c.ND):
                            gidx = qb * c.ND + d
                            if gidx + 1 < c.NQB * c.ND:
                                wkey(tensor, ("wo_w", gidx + 1))
                            else:
                                wkey(tensor, ("wo_can",))
                            if d >= 2:
                                vC4.wait(tensor, vC40 + qb * c.ND + d - 1)
                                sD1.wait(tensor, sD10 + qb * c.ND + d - 1)
                            if ST("wo"):
                                i = tensor.nop()
                            else:
                                for dh in range(c.HQ):
                                    i = tensor.matmul(
                                        ps_wo[d % 2][:, :],
                                        wobuf[:, (d % 2) * c.HQ * 128 + dh * 128:
                                              (d % 2) * c.HQ * 128 + (dh + 1) * 128],
                                        attnf[:, dh * Q:(dh + 1) * Q],
                                        start=(dh == 0), stop=(dh == c.HQ - 1))
                            st.inc(tensor.drain(fusable=False))
                            mark(("womm", qb, d), st)
                            tA.inc(tensor.drain(fusable=False))
                        if ST("dumpp") and qb == c.NQB - 1:
                            wkey(tensor, ("pdump",))
                        sD1.wait(tensor, sD10 + (qb + 1) * c.ND)
                        if ST("ln2mm"):
                            i = tensor.nop()
                        else:
                            for d in range(c.ND):
                                i = tensor.matmul(ps_l[0][:, :], ones_col[:, :],
                                                  probs[:, (c.ND + d) * Q:(c.ND + d + 1) * Q],
                                                  start=(d == 0), stop=(d == c.ND - 1))
                        st.inc(tensor.drain(fusable=False))
                        mark(("ln2_s0", qb), st)
                        vC1.wait(tensor, vC1_base(qb) + c.HQ * c.NDIAG + c.ND)
                        if ST("ln2mm"):
                            i = tensor.nop()
                        else:
                            for d in range(c.ND):
                                i = tensor.matmul(ps_l[1][:, :], ones_col[:, :],
                                                  probs[:, d * Q:(d + 1) * Q],
                                                  start=(d == 0), stop=(d == c.ND - 1))
                        st.inc(tensor.drain(fusable=False))
                        mark(("ln2_s1", qb), st)
                        vC6.wait(tensor, vC60 + qb + 1)
                        sD2.wait(tensor, sD20 + qb + 1)
                        if ST("ln2mm"):
                            st.inc(tensor.nop())
                            st.inc(tensor.nop())
                        else:
                            i = tensor.matmul(ps_sc[0][:, :], ones_row[:, :], rows[:, 0:Q],
                                              start=True, stop=True)
                            st.inc(tensor.drain(fusable=False))
                            i = tensor.matmul(ps_sc[1][:, :], ones_row[:, :],
                                              rows[:, 2 * Q:3 * Q], start=True, stop=True)
                            st.inc(tensor.drain(fusable=False))
                        mark(("ln2_bc", qb), st)

                @block.scalar
                def _(scalar):
                    join_all(scalar)
                    for qb in range(c.NQB):
                        kc = kcnt(qb)
                        if qb >= 1:
                            wkey(scalar, ("ln2_s1", qb - 1))
                        for kv in range(c.HKV):
                            for j in range(c.G):
                                h = kv * c.G + j
                                if kv >= 1:
                                    wkey(scalar, ("pv", qb, h - c.G))
                                for kj in range(kc):
                                    wkey(scalar, ("sc", qb, h, kj))
                                    if ST("exp"):
                                        i = scalar.nop()
                                    else:
                                        i = scalar.activation(
                                            probs[:, (kj * c.G + j) * Q:
                                                  (kj * c.G + j + 1) * Q],
                                            ps_sc[kj % 2][:, :], AF.Exp, scale=sc_scale)
                                    sB.inc(scalar.drain(fusable=False))
                        if ST("dumpp") and qb == c.NQB - 1:
                            wkey(scalar, ("pdump",))
                        for d in range(c.ND):
                            vC4.wait(scalar, vC40 + qb * c.ND + d + 1)
                            if ST("swo"):
                                i = scalar.nop()
                            else:
                                i = scalar.copy(probs[:, (c.ND + d) * Q:(c.ND + d + 1) * Q],
                                                wo_sb[:, d * Q:(d + 1) * Q])
                            sD1.inc(scalar.drain(fusable=False))
                        vC6.wait(scalar, vC60 + qb + 1)
                        if ST("srows"):
                            i = scalar.nop()
                        else:
                            scalar.activation(rows[:, 3 * Q:4 * Q],
                                              rows[:, 2 * Q:3 * Q], AF.Ln)
                            scalar.drain(fusable=False)
                            i = scalar.activation(rows[:, 2 * Q:3 * Q], rows[:, 3 * Q:4 * Q],
                                                  AF.Exp, scale=-0.5)
                        sD2.inc(scalar.drain(fusable=False))

                @block.vector
                def _(vector):
                    join_all(vector)
                    wkey(vector, ("masks",))
                    for qb in range(c.NQB):
                        kc = kcnt(qb)
                        for kv in range(c.HKV):
                            for j in range(c.G):
                                h = kv * c.G + j
                                ho = qb * c.HQ + h
                                if ho >= 2:
                                    pkey = ("pv", (ho - 2) // c.HQ, (ho - 2) % c.HQ)
                                    if pkey in M:
                                        wkey(vector, pkey)
                                for m in range(c.NDIAG):
                                    kj = qb * c.NDIAG + m
                                    eidx = sB0 + exps_before(qb) + h * kc + kj
                                    sB.wait(vector, eidx + 1)
                                    psl = probs[:, (kj * c.G + j) * Q:
                                                (kj * c.G + j + 1) * Q]
                                    msl = mscr[:, ((h % 2) * c.NDIAG + m) * Q:
                                               ((h % 2) * c.NDIAG + m + 1) * Q]
                                    if ST("vmask"):
                                        i = vector.engine_nop()
                                    else:
                                        i = vector.tensor_mul(msl, psl,
                                                              masks[:, m * Q:(m + 1) * Q])
                                    vC1.inc(vector.drain(fusable=False))
                                wkey(vector, ("lmm", qb, h))
                                ls = slice((j % 2) * Q, (j % 2) * Q + Q)
                                if ST("vl"):
                                    i = vector.engine_nop()
                                else:
                                    vector.tensor_copy(lrow[:, 0:Q], ps_l[j % 2][:, :])
                                    vector.drain(fusable=False)
                                    vector.reciprocal(lrow[:, 0:Q], lrow[:, 0:Q])
                                    vector.drain(fusable=False)
                                    i = vector.tensor_copy(rlbf[:, ls], lrow[:, 0:Q])
                                vC2.inc(vector.drain(fusable=False))
                                wkey(vector, ("lbc", qb, h))
                                wkey(vector, ("pv", qb, h))
                                if ST("vevac"):
                                    i = vector.engine_nop()
                                else:
                                    vector.tensor_copy(rlbc[:, ls], ps_bc[:, :])
                                    i = vector.tensor_tensor(attnf[:, h * Q:(h + 1) * Q],
                                                             ps_pv[:, :], rlbc[:, ls],
                                                             ALU.mult)
                                vC3.inc(vector.drain(fusable=False))
                        for d in range(c.ND):
                            wkey(vector, ("womm", qb, d))
                            if ST("vwo"):
                                i = vector.engine_nop()
                            else:
                                i = vector.tensor_copy(wo_sb[:, d * Q:(d + 1) * Q],
                                                       ps_wo[d % 2][:, :])
                            vC4.inc(vector.drain(fusable=False))
                        if ST("dumpp") and qb == c.NQB - 1:
                            wkey(vector, ("pdump",))
                        for d in range(c.ND):
                            sD1.wait(vector, sD10 + qb * c.ND + d + 1)
                            if ST("vsq"):
                                i = vector.engine_nop()
                            else:
                                i = vector.tensor_mul(probs[:, d * Q:(d + 1) * Q],
                                                      probs[:, (c.ND + d) * Q:(c.ND + d + 1) * Q],
                                                      probs[:, (c.ND + d) * Q:(c.ND + d + 1) * Q])
                            vC1.inc(vector.drain(fusable=False))
                        wkey(vector, ("ln2_s1", qb))
                        if qb >= 1:
                            sD2.wait(vector, sD20 + qb)
                        if ST("vrows"):
                            i = vector.engine_nop()
                        else:
                            vector.tensor_scalar_mul(rows[:, 0:Q], ps_l[0][:, :], inv_d)
                            vector.tensor_scalar_mul(rows[:, Q:2 * Q], ps_l[1][:, :], inv_d)
                            vector.drain(fusable=False)
                            vector.tensor_mul(rows[:, 3 * Q:4 * Q], rows[:, 0:Q], rows[:, 0:Q])
                            vector.drain(fusable=False)
                            vector.tensor_sub(rows[:, 2 * Q:3 * Q], rows[:, Q:2 * Q],
                                              rows[:, 3 * Q:4 * Q])
                            vector.drain(fusable=False)
                            i = vector.tensor_scalar_add(rows[:, 2 * Q:3 * Q],
                                                         rows[:, 2 * Q:3 * Q], c.eps)
                        vC6.inc(vector.drain(fusable=False))
                        wkey(vector, ("ln2_bc", qb))
                        for d in range(c.ND):
                            wkey(vector, ("a_x", qb, d))
                            gi = qb * c.ND + d
                            if gi >= 2:
                                wkey(vector, ("a_x2", (gi - 2) // c.ND, (gi - 2) % c.ND))
                            if ST("vapply"):
                                i = vector.engine_nop()
                            else:
                                vector.tensor_sub(wo_sb[:, d * Q:(d + 1) * Q],
                                                  wo_sb[:, d * Q:(d + 1) * Q], ps_sc[0][:, :])
                                vector.tensor_tensor(wo_sb[:, d * Q:(d + 1) * Q],
                                                     wo_sb[:, d * Q:(d + 1) * Q],
                                                     ps_sc[1][:, :], ALU.mult)
                                i = vector.tensor_add(x2st[:, (d % 2) * Q:(d % 2) * Q + Q],
                                                      wo_sb[:, d * Q:(d + 1) * Q],
                                                      xst[:, (d % 2) * Q:(d % 2) * Q + Q])
                            vC7.inc(vector.drain(fusable=False))

        def emit_ffn():
            join_all = make_join(phase_snap())
            T = c.TBLK
            NTB = c.NTB
            with ExitStack() as px:
                fcanary = px.enter_context(nc.sbuf_tensor("f_canary", [128, 16], BF16))
                xn3 = px.enter_context(nc.sbuf_tensor("f_x", [128, c.ND * T], BF16))
                wgb = px.enter_context(nc.sbuf_tensor("f_wg", [128, 2 * c.ND * 128], BF16))
                wub = px.enter_context(nc.sbuf_tensor("f_wu", [128, 2 * c.ND * 128], BF16))
                wdb = px.enter_context(nc.sbuf_tensor("f_wd", [128, 2 * c.NF * 128], BF16))
                silu_s = px.enter_context(nc.sbuf_tensor("f_si", [128, 2 * T], F32))
                mid = px.enter_context(nc.sbuf_tensor("f_mid", [128, c.NF * T], BF16))
                y_s = px.enter_context(nc.sbuf_tensor("f_y", [128, c.ND * T], F32))
                y_bf = px.enter_context(nc.sbuf_tensor("f_ybf", [128, c.ND * T], BF16))
                x2t = px.enter_context(nc.sbuf_tensor("f_x2", [128, 2 * T], F32))
                ost = px.enter_context(nc.sbuf_tensor("f_o", [128, 2 * T], F32))
                rows = px.enter_context(nc.sbuf_tensor("f_rows", [1, 4 * T], F32))

                ps_g = [px.enter_context(nc.psum_tensor(f"f_g{i}", [128, T], F32))
                        for i in range(2)]
                ps_u = [px.enter_context(nc.psum_tensor(f"f_u{i}", [128, T], F32))
                        for i in range(2)]
                ps_dn = [px.enter_context(nc.psum_tensor(f"f_d{i}", [128, T], F32))
                         for i in range(2)]
                ps_st = [px.enter_context(nc.psum_tensor(f"f_s{i}", [1, T], F32))
                         for i in range(2)]

                tA0, sB0, sD10, sD20 = tA.n, sB.n, sD1.n, sD2.n
                vC10, vC20, vC30, vC40, vC70 = vC1.n, vC2.n, vC3.n, vC4.n, vC7.n

                def tA_up(tb, hc):
                    return tA0 + tb * (c.NF + c.ND) + hc + 1

                def tA_dn(tb, dc):
                    return tA0 + tb * (c.NF + c.ND) + c.NF + dc + 1

                @block.gpsimd
                def _(gpsimd):
                    join_all(gpsimd)
                    for tb in range(NTB):
                        for hc in range(c.NF):
                            gi = tb * c.NF + hc
                            if gi >= 2:
                                ptb, phc = (gi - 2) // c.NF, (gi - 2) % c.NF
                                tA.wait(gpsimd, tA_up(ptb, phc))
                            i = gpsimd.dma_start(
                                out=bass.AP(wgb, (hc % 2) * c.ND * 128,
                                            [[2 * c.ND * 128, 128], [128, c.ND], [1, 128]]),
                                in_=bass.AP(wg_bf, hc * c.D * 128,
                                            [[c.ND * 128, 128], [128, c.ND], [1, 128]]))
                            dg.inc(i)
                            i = gpsimd.dma_start(
                                out=bass.AP(wub, (hc % 2) * c.ND * 128,
                                            [[2 * c.ND * 128, 128], [128, c.ND], [1, 128]]),
                                in_=bass.AP(wu_bf, hc * c.D * 128,
                                            [[c.ND * 128, 128], [128, c.ND], [1, 128]]))
                            dg.inc(i)
                            mark(("f_w", gi), dg)
                        for dc in range(c.ND):
                            gi = tb * c.ND + dc
                            if gi >= 2:
                                ptb, pdc = (gi - 2) // c.ND, (gi - 2) % c.ND
                                tA.wait(gpsimd, tA_dn(ptb, pdc))
                            i = gpsimd.dma_start(
                                out=bass.AP(wdb, (dc % 2) * c.NF * 128,
                                            [[2 * c.NF * 128, 128], [128, c.NF], [1, 128]]),
                                in_=bass.AP(wd_bf, dc * c.F * 128,
                                            [[c.NF * 128, 128], [128, c.NF], [1, 128]]))
                            dg.inc(i)
                            mark(("f_wd", gi), dg)
                        i = gpsimd.dma_start(
                            out=fcanary[:, :],
                            in_=bass.AP(wd_bf, (c.ND - 1) * c.F * 128,
                                        [[c.NF * 128, 128], [1, 16]]))
                        dg.inc(i)
                        mark(("f_can", tb), dg)

                @block.sync
                def _(sync):
                    join_all(sync)
                    for tb in range(NTB):
                        if tb >= 1:
                            tA.wait(sync, tA_up(tb - 1, c.NF - 1))
                        i = sync.dma_start(
                            out=bass.AP(xn3, 0, [[c.ND * T, 128], [T, c.ND], [1, T]]),
                            in_=bass.AP(xn3_sc, tb * T,
                                        [[c.S, 128], [128 * c.S, c.ND], [1, T]]))
                        dl.inc(i)
                        mark(("f_xn3", tb), dl)
                        for d in range(c.ND):
                            gi = tb * c.ND + d
                            if gi >= 2:
                                vC7.wait(sync, vC70 + gi - 1)
                            i = sync.dma_start(out=x2t[:, (d % 2) * T:(d % 2) * T + T],
                                               in_=x2_sc[d, :, tb * T:(tb + 1) * T])
                            dl.inc(i)
                            mark(("f_x2", tb, d), dl)
                            vC7.wait(sync, vC70 + gi + 1)
                            i = sync.dma_start(out=out_h[d, :, tb * T:(tb + 1) * T],
                                               in_=ost[:, (d % 2) * T:(d % 2) * T + T])
                            dl.inc(i)
                            mark(("f_out", tb, d), dl)
                    dl.wait(sync)

                @block.tensor
                def _(tensor):
                    join_all(tensor)
                    for tb in range(NTB):
                        wkey(tensor, ("f_xn3", tb))
                        for hc in range(c.NF):
                            gi = tb * c.NF + hc
                            if hc + 1 < c.NF:
                                wkey(tensor, ("f_w", gi + 1))
                            else:
                                wkey(tensor, ("f_wd", tb * c.ND))
                            if gi >= 2:
                                sB.wait(tensor, sB0 + gi - 1)
                                vC1.wait(tensor, vC10 + gi - 1)
                            if tb >= 1 and hc < 2:
                                vC7.wait(tensor, vC70 + tb * c.ND)
                            if tb >= 1 and hc == 0:
                                pass
                            for d in range(c.ND):
                                i = tensor.matmul(
                                    ps_g[hc % 2][:, :],
                                    wgb[:, (hc % 2) * c.ND * 128 + d * 128:
                                        (hc % 2) * c.ND * 128 + (d + 1) * 128],
                                    xn3[:, d * T:(d + 1) * T],
                                    start=(d == 0), stop=(d == c.ND - 1))
                            st.inc(tensor.drain(fusable=False))
                            mark(("f_g", gi), st)
                            for d in range(c.ND):
                                i = tensor.matmul(
                                    ps_u[hc % 2][:, :],
                                    wub[:, (hc % 2) * c.ND * 128 + d * 128:
                                        (hc % 2) * c.ND * 128 + (d + 1) * 128],
                                    xn3[:, d * T:(d + 1) * T],
                                    start=(d == 0), stop=(d == c.ND - 1))
                            st.inc(tensor.drain(fusable=False))
                            mark(("f_u", gi), st)
                            tA.inc(tensor.drain(fusable=False))
                        vC1.wait(tensor, vC10 + (tb + 1) * c.NF)
                        for dc in range(c.ND):
                            gi = tb * c.ND + dc
                            if dc + 1 < c.ND:
                                wkey(tensor, ("f_wd", gi + 1))
                            else:
                                wkey(tensor, ("f_can", tb))
                            if dc >= 2:
                                vC2.wait(tensor, vC20 + tb * c.ND + dc - 1)
                                sD1.wait(tensor, sD10 + tb * c.ND + dc - 1)
                            for hc in range(c.NF):
                                i = tensor.matmul(
                                    ps_dn[dc % 2][:, :],
                                    wdb[:, (dc % 2) * c.NF * 128 + hc * 128:
                                        (dc % 2) * c.NF * 128 + (hc + 1) * 128],
                                    mid[:, hc * T:(hc + 1) * T],
                                    start=(hc == 0), stop=(hc == c.NF - 1))
                            st.inc(tensor.drain(fusable=False))
                            mark(("f_dn", gi), st)
                            tA.inc(tensor.drain(fusable=False))
                        sD1.wait(tensor, sD10 + (tb + 1) * c.ND)
                        if tb >= 1:
                            vC4.wait(tensor, vC40 + tb)
                        for d in range(c.ND):
                            i = tensor.matmul(ps_st[0][:, :], ones_col[:, :],
                                              y_bf[:, d * T:(d + 1) * T],
                                              start=(d == 0), stop=(d == c.ND - 1))
                        st.inc(tensor.drain(fusable=False))
                        mark(("f_s0", tb), st)
                        vC3.wait(tensor, vC30 + tb + 1)
                        for d in range(c.ND):
                            i = tensor.matmul(ps_st[1][:, :], ones_col[:, :],
                                              mid[:, d * T:(d + 1) * T],
                                              start=(d == 0), stop=(d == c.ND - 1))
                        st.inc(tensor.drain(fusable=False))
                        mark(("f_s1", tb), st)
                        vC4.wait(tensor, vC40 + tb + 1)
                        sD2.wait(tensor, sD20 + tb + 1)
                        i = tensor.matmul(ps_g[0][:, :], ones_row[:, :], rows[:, 0:T],
                                          start=True, stop=True)
                        st.inc(tensor.drain(fusable=False))
                        i = tensor.matmul(ps_g[1][:, :], ones_row[:, :],
                                          rows[:, 2 * T:3 * T], start=True, stop=True)
                        st.inc(tensor.drain(fusable=False))
                        mark(("f_bc", tb), st)

                @block.scalar
                def _(scalar):
                    join_all(scalar)
                    for tb in range(NTB):
                        for hc in range(c.NF):
                            gi = tb * c.NF + hc
                            wkey(scalar, ("f_g", gi))
                            if gi >= 2:
                                vC1.wait(scalar, vC10 + gi - 1)
                            i = scalar.activation(silu_s[:, (hc % 2) * T:(hc % 2) * T + T],
                                                  ps_g[hc % 2][:, :], AF.Silu)
                            sB.inc(scalar.drain(fusable=False))
                        for d in range(c.ND):
                            vC2.wait(scalar, vC20 + tb * c.ND + d + 1)
                            i = scalar.copy(y_bf[:, d * T:(d + 1) * T],
                                            y_s[:, d * T:(d + 1) * T])
                            sD1.inc(scalar.drain(fusable=False))
                        vC4.wait(scalar, vC40 + tb + 1)
                        scalar.activation(rows[:, 3 * T:4 * T],
                                          rows[:, 2 * T:3 * T], AF.Ln)
                        scalar.drain(fusable=False)
                        i = scalar.activation(rows[:, 2 * T:3 * T], rows[:, 3 * T:4 * T],
                                              AF.Exp, scale=-0.5)
                        sD2.inc(scalar.drain(fusable=False))

                @block.vector
                def _(vector):
                    join_all(vector)
                    for tb in range(NTB):
                        for hc in range(c.NF):
                            gi = tb * c.NF + hc
                            sB.wait(vector, sB0 + gi + 1)
                            wkey(vector, ("f_u", gi))
                            if tb >= 1 and hc < c.ND:
                                wkey(vector, ("f_s1", tb - 1))
                            i = vector.tensor_tensor(mid[:, hc * T:(hc + 1) * T],
                                                     ps_u[hc % 2][:, :],
                                                     silu_s[:, (hc % 2) * T:(hc % 2) * T + T],
                                                     ALU.mult)
                            vC1.inc(vector.drain(fusable=False))
                        for d in range(c.ND):
                            wkey(vector, ("f_dn", tb * c.ND + d))
                            i = vector.tensor_copy(y_s[:, d * T:(d + 1) * T],
                                                   ps_dn[d % 2][:, :])
                            vC2.inc(vector.drain(fusable=False))
                        wkey(vector, ("f_dn", tb * c.ND + c.ND - 1))
                        for d in range(c.ND):
                            sD1.wait(vector, sD10 + tb * c.ND + d + 1)
                            i = vector.tensor_mul(mid[:, d * T:(d + 1) * T],
                                                  y_bf[:, d * T:(d + 1) * T],
                                                  y_bf[:, d * T:(d + 1) * T])
                        vC3.inc(vector.drain(fusable=False))
                        wkey(vector, ("f_s1", tb))
                        if tb >= 1:
                            sD2.wait(vector, sD20 + tb)
                        vector.tensor_scalar_mul(rows[:, 0:T], ps_st[0][:, :], inv_d)
                        vector.tensor_scalar_mul(rows[:, T:2 * T], ps_st[1][:, :], inv_d)
                        vector.drain(fusable=False)
                        vector.tensor_mul(rows[:, 3 * T:4 * T], rows[:, 0:T], rows[:, 0:T])
                        vector.drain(fusable=False)
                        vector.tensor_sub(rows[:, 2 * T:3 * T], rows[:, T:2 * T],
                                          rows[:, 3 * T:4 * T])
                        vector.drain(fusable=False)
                        i = vector.tensor_scalar_add(rows[:, 2 * T:3 * T],
                                                     rows[:, 2 * T:3 * T], c.eps)
                        vC4.inc(vector.drain(fusable=False))
                        wkey(vector, ("f_bc", tb))
                        for d in range(c.ND):
                            wkey(vector, ("f_x2", tb, d))
                            gi = tb * c.ND + d
                            if gi >= 2:
                                wkey(vector, ("f_out", (gi - 2) // c.ND, (gi - 2) % c.ND))
                            vector.tensor_sub(y_s[:, d * T:(d + 1) * T],
                                              y_s[:, d * T:(d + 1) * T], ps_g[0][:, :])
                            vector.tensor_tensor(y_s[:, d * T:(d + 1) * T],
                                                 y_s[:, d * T:(d + 1) * T],
                                                 ps_g[1][:, :], ALU.mult)
                            i = vector.tensor_add(ost[:, (d % 2) * T:(d % 2) * T + T],
                                                  y_s[:, d * T:(d + 1) * T],
                                                  x2t[:, (d % 2) * T:(d % 2) * T + T])
                            vC7.inc(vector.drain(fusable=False))

        with ExitStack() as p12:
            kcache = p12.enter_context(nc.sbuf_tensor("kcache", [128, c.HKV * c.S], BF16))
            vcache = p12.enter_context(nc.sbuf_tensor("vcache", [128, c.NT * c.KVD], BF16))
            with ExitStack() as p01:
                xn1 = p01.enter_context(nc.sbuf_tensor("xn1", [128, c.ND * c.S], BF16))
                if "ln1" in phases:
                    emit_ln("ln1", x_h, xn1, None)
                if "qkv" in phases:
                    emit_qkv(xn1, kcache, vcache)
            if "attn" in phases:
                emit_attn(kcache, vcache)
        if "ln3" in phases:
            emit_ln("ln3", x2_sc, None, xn3_sc)
        if "ffn" in phases:
            emit_ffn()
        if "dumpx2" in phases:
            snap2 = phase_snap()

            @block.sync
            def _(sync):
                make_join(snap2)(sync)
                i = sync.dma_start(out=dbg_x2o[:, :, :], in_=x2_sc[:, :, :])
                dl.inc(i)
                i = sync.dma_start(out=dbg_x3o[:, :, :], in_=xn3_sc[:, :, :])
                dl.inc(i)
                i = sync.dma_start(out=dbg_qo[:, :, :], in_=q_sc[:, :, :])
                dl.inc(i)
                dl.wait(sync)
        if "ffn" not in phases:
            snap = phase_snap()

            @block.sync
            def _(sync):
                make_join(snap)(sync)
                if "attn" in phases:
                    i = sync.dma_start(out=out_h[:, :, :], in_=x2_sc[:, :, :])
                    dl.inc(i)
                elif "ln3" in phases:
                    i = sync.dma_start(out=out_h[:, :, :], in_=x2_sc[:, :, :])
                    dl.inc(i)
                if dbg_q is not None:
                    i = sync.dma_start(out=dbg_q[:, :, :], in_=q_sc[:, :, :])
                    dl.inc(i)
                if dbg_x3 is not None:
                    i = sync.dma_start(out=dbg_x3[:, :, :], in_=xn3_sc[:, :, :])
                    dl.inc(i)
                dl.wait(sync)

    return nc



def _host_constants(c: Cfg):
    i = np.arange(0, c.DH, 2, dtype=np.float64) / c.DH
    freqs = 1.0 / (c.theta ** i)
    t = np.arange(c.S, dtype=np.float64)
    ang = np.outer(freqs, t)
    cos = np.cos(ang)
    sin = np.sin(ang)
    c1 = np.concatenate([cos, cos], axis=0).astype(np.float32)
    c2 = np.concatenate([-sin, sin], axis=0).astype(np.float32)
    masks = np.zeros((c.NDIAG, 128, c.QBLK), dtype=np.float32)
    r = np.arange(128)[:, None]
    q = np.arange(c.QBLK)[None, :]
    for j in range(c.NDIAG):
        masks[j] = (128 * j + r <= q).astype(np.float32)
    return c1, c2, masks.astype(ml_dtypes.bfloat16)


def _prep_weights(c, wqkv, wo, wg, wu, wd):
    def colblocks(w, nb):
        K = w.shape[0]
        a = w.reshape(K // 128, 128, nb, 128)
        return np.ascontiguousarray(a.transpose(2, 1, 0, 3))
    wqkv_r = colblocks(wqkv, c.NQKV)
    wo_r = colblocks(wo, c.ND)
    wg_r = colblocks(wg.T, c.NF)
    wu_r = colblocks(wu.T, c.NF)
    wd_r = colblocks(wd.T, c.ND)
    return wqkv_r, wo_r, wg_r, wu_r, wd_r


def make_in_maps(c, x, wqkv, wo, wg, wu, wd):
    c1, c2, masks = _host_constants(c)
    wqkv_r, wo_r, wg_r, wu_r, wd_r = _prep_weights(c, wqkv, wo, wg, wu, wd)
    in_maps = []
    for b in range(x.shape[0]):
        xb = np.ascontiguousarray(x[b].T).reshape(c.ND, 128, c.S)
        in_maps.append({
            "x": xb, "wqkv_r": wqkv_r, "wo_r": wo_r,
            "wg_r": wg_r, "wu_r": wu_r, "wd_r": wd_r,
            "rope_c1": c1, "rope_c2": c2, "masks": masks,
        })
    return in_maps


def kernel(L0_attn_x, L0_attn_Wqkv, L0_attn_Wo, L0_ffn_w_gate, L0_ffn_w_up,
           L0_ffn_w_down):
    c = Cfg()
    x = np.asarray(L0_attn_x)
    B = x.shape[0]
    in_maps = make_in_maps(c, x, np.asarray(L0_attn_Wqkv), np.asarray(L0_attn_Wo),
                           np.asarray(L0_ffn_w_gate), np.asarray(L0_ffn_w_up),
                           np.asarray(L0_ffn_w_down))
    nc = build_core_program(c)
    res = run_bass_kernel_spmd(nc, in_maps, core_ids=list(range(B)))
    outs = []
    for b in range(B):
        ob = np.asarray(res.results[b]["out"]).reshape(c.D, c.S)
        outs.append(ob.T)
    return np.ascontiguousarray(np.stack(outs)).astype(np.float32)
